# version 08
# baseline (speedup 1.0000x reference)
import contextlib

import numpy as np
import ml_dtypes

import concourse.bass as bass
import concourse.mybir as mybir
from concourse.bass_utils import run_bass_kernel_spmd

F32 = mybir.dt.float32
BF16 = mybir.dt.bfloat16
I32 = mybir.dt.int32
AF = mybir.ActivationFunctionType
ALU = mybir.AluOpType
AX = mybir.AxisListType

B, N, D = 4, 4096, 1024
H = 16
FF = 4096
NT = N // 128
MYT = NT // 2
MYN = MYT * 128
NG = 4
SLOPES = [2.0 ** (-0.5 * (i + 1)) for i in range(H)]
MSHIFT = 12.0


def build_nc():
    nc = bass.Bass("TRN2", target_bir_lowering=False)
    ap = lambda nm, shp, ty, kind: nc.dram_tensor(nm, shp, ty, kind=kind).ap()

    x_full = ap("x_full", [N, D], F32, "ExternalInput")
    x_my = ap("x_my", [MYN, D], F32, "ExternalInput")
    Wq = ap("Wq", [D, D], F32, "ExternalInput")
    Wk = ap("Wk", [D, D], F32, "ExternalInput")
    Wv = ap("Wv", [D, D], F32, "ExternalInput")
    Wo = ap("Wo", [D, D], F32, "ExternalInput")
    bo = ap("bo", [D], F32, "ExternalInput")
    w1d = ap("w1", [D, FF], F32, "ExternalInput")
    b1d = ap("b1", [128, FF // 128], F32, "ExternalInput")
    w2d = ap("w2", [FF, D], F32, "ExternalInput")
    b2d = ap("b2", [D], F32, "ExternalInput")
    ident_in = ap("identc", [128, 128], BF16, "ExternalInput")
    kpos_in = ap("kposc", [128, NT], F32, "ExternalInput")
    qshift_in = ap("qshiftc", [128, H * NG], F32, "ExternalInput")
    masks_in = ap("masksc", [128, 2 * 128], BF16, "ExternalInput")
    qpr_in = ap("qprc", [128, 512], F32, "ExternalInput")
    ladc_in = ap("ladc", [128, 64], F32, "ExternalInput")
    out_d = ap("out", [MYN, D], F32, "ExternalOutput")
    x2_d = ap("x2scr", [MYN, D], F32, "Internal")
    ctx_d = ap("ctxscr", [H * 64, MYN], BF16, "Internal")
    w1bf = ap("w1bf", [128, 32 * 1024], BF16, "Internal")
    w2bf = ap("w2bf", [128, 32 * 1024], BF16, "Internal")
    wobf = ap("wobf", [64, 16 * 1024], BF16, "Internal")

    ops = {e: [] for e in ("sync", "tensor", "vector", "scalar", "gpsimd")}
    cnt = {s: 0 for s in ("t", "v", "s", "cl", "wx", "wo",
                          "x0", "x1", "cx0", "cx1", "w1a", "w1b", "w2a", "w2b",
                          "x2a", "x2b", "outa", "outb", "ctxa", "ctxb",
                          "wca", "wcb", "wcsa", "wcsb")}
    semh = {}

    def op(eng, fn, sem=None):
        if sem is None:
            ops[eng].append(fn)
            return None
        cnt[sem] += 1
        tgt = cnt[sem]
        ops[eng].append(lambda io, f=fn, ss=sem: f(io).then_inc(semh[ss], 1))
        return tgt

    def dma(eng, dst, src, sem):
        cnt[sem] += 16
        tgt = cnt[sem]
        ops[eng].append(lambda io, d=dst, s=src, ss=sem:
                        io.dma_start(out=d, in_=s).then_inc(semh[ss], 16))
        return tgt

    def wait(eng, sem, tgt):
        if tgt:
            ops[eng].append(lambda io, ss=sem, t=tgt: io.wait_ge(semh[ss], t))

    def drain(eng):
        ops[eng].append(lambda io: io.drain())

    stk = contextlib.ExitStack()
    e = stk.enter_context
    with stk:
        ident = e(nc.sbuf_tensor("identt", [128, 128], BF16))
        kpos = e(nc.sbuf_tensor("kpost", [128, NT], F32))
        qshift = e(nc.sbuf_tensor("qshiftt", [128, H * NG], F32))
        masks = e(nc.sbuf_tensor("maskst", [128, 2 * 128], BF16))
        qpr = e(nc.sbuf_tensor("qprt", [128, 512], F32))
        ladc = e(nc.sbuf_tensor("ladct", [128, 64], F32))
        crow = e(nc.sbuf_tensor("crowt", [128, 512], F32))
        biasmat = e(nc.sbuf_tensor("biasmatt", [128, NT], F32))
        b1bias = e(nc.sbuf_tensor("b1biast", [128, FF // 128], F32))
        brow = e(nc.sbuf_tensor("browt", [128, D], F32))
        onesc = e(nc.sbuf_tensor("onesct", [128, 128], BF16))
        onescf = e(nc.sbuf_tensor("onescft", [128, 128], F32))
        cbias = e(nc.sbuf_tensor("cbiast", [128, 8], F32))
        xstage = [e(nc.sbuf_tensor(f"xstaget{i}", [128, D], F32)) for i in range(2)]
        xl = e(nc.sbuf_tensor("xlt", [128, D], BF16))
        stats = e(nc.sbuf_tensor("statst", [128, 8], F32))
        sqscr = e(nc.sbuf_tensor("sqscrt", [128, D], F32))
        wosum = e(nc.sbuf_tensor("wosumt", [128, D], F32))
        lntmp = e(nc.sbuf_tensor("lntmpt", [128, D], F32))
        x2t = [e(nc.sbuf_tensor(f"x2tt{i}", [128, D], F32)) for i in range(2)]
        outt = [e(nc.sbuf_tensor(f"outtt{i}", [128, D], F32)) for i in range(2)]

        stk1 = contextlib.ExitStack()
        e1 = stk1.enter_context
        Ksb = [e1(nc.sbuf_tensor(f"Ksb{i}", [128, N], BF16)) for i in range(4)]
        Qsb = [e1(nc.sbuf_tensor(f"Qsb{i}", [128, MYN], BF16)) for i in range(4)]
        Vsb = [e1(nc.sbuf_tensor(f"Vsb{i}", [128, NT * 65], BF16)) for i in range(8)]
        Wx3 = e1(nc.sbuf_tensor("Wx3t", [128, 24 * 512], BF16))
        xlT = e1(nc.sbuf_tensor("xlTt", [128, 8 * 512], BF16))
        psb = [e1(nc.sbuf_tensor(f"psbt{i}", [128, 512], BF16)) for i in range(3)]
        zrow = e1(nc.sbuf_tensor("zrowt", [128, 512], F32))
        rzrow = e1(nc.sbuf_tensor("rzrowt", [128, 512], F32))
        rzbc = e1(nc.sbuf_tensor("rzbct", [64, 512], F32))
        ctxst = [e1(nc.sbuf_tensor(f"ctxstt{i}", [64, 512], BF16)) for i in range(2)]
        wstg = [e1(nc.sbuf_tensor(f"wstgt{i}", [128, 1024], F32)) for i in range(2)]
        wstgb = [e1(nc.sbuf_tensor(f"wstgbt{i}", [128, 1024], BF16)) for i in range(2)]

        pstk1 = contextlib.ExitStack()
        ep1 = pstk1.enter_context
        pS = [ep1(nc.psum_tensor(f"pSp{i}", [128, 512], F32)) for i in range(3)]
        pC = ep1(nc.psum_tensor("pCp", [65, 512], F32))
        pZ = ep1(nc.psum_tensor("pZp", [64, 512], F32))
        pT = ep1(nc.psum_tensor("pTp", [128, D], BF16))
        pP = [ep1(nc.psum_tensor(f"pPp{i}", [128, 512], F32)) for i in range(2)]

        for snm in cnt:
            semh[snm] = e(nc.semaphore(f"sem_{snm}"))
        BLOCK = e(nc.Block())

        t_c = 0
        for dst, src in ((ident, ident_in), (kpos, kpos_in), (qshift, qshift_in),
                         (masks, masks_in), (qpr, qpr_in), (ladc, ladc_in)):
            t_c = dma("sync", dst[:], src, "cl")
        t_c = dma("sync", brow[0:1, :], bo.rearrange("(o d) -> o d", o=1), "cl")
        t_c = dma("sync", brow[64:65, :], b2d.rearrange("(o d) -> o d", o=1), "cl")
        t_c = dma("sync", b1bias[:], b1d, "cl")
        t_consts = t_c

        wait("vector", "cl", t_consts)
        op("vector", lambda io: io.memset(onesc[:], 1.0))
        op("vector", lambda io: io.memset(onescf[:], 1.0))
        for col, val in ((0, 1e-5), (1, -7.125), (2, -6.875), (3, 64.5),
                         (4, 8192.5), (5, -0.220703125)):
            op("vector", lambda io, C=col, VV=val: io.memset(cbias[:, C:C + 1], VV))
        for hh in range(8):
            op("vector", lambda io, h=hh: io.memset(
                Vsb[h][:].rearrange("p (k c) -> p k c", c=65)[:, :, 64:65], 1.0))
        t_b1b = op("vector", lambda io: io.tensor_scalar(
            out=b1bias[:], in0=b1bias[:], scalar1=256.0, scalar2=2048.5,
            op0=ALU.mult, op1=ALU.add), sem="v")
        drain("vector")

        def bcast_row(r_i, dstt, pbank):
            for half in range(2):
                for prange in range(2):
                    t_bc = op("tensor", lambda io, rr=r_i, hh=half, pr=prange, pb=pbank:
                              io.matmul(pb[0:64, 0:512],
                                        onescf[rr:rr + 1, pr * 64:(pr + 1) * 64],
                                        brow[rr:rr + 1, hh * 512:(hh + 1) * 512],
                                        start=True, stop=True,
                                        tile_position=(rr, 0)), sem="t")
                    wait("vector", "t", t_bc)
                    t_bcv = op("vector", lambda io, d=dstt, hh=half, pr=prange, pb=pbank:
                               io.tensor_copy(
                                   out=d[pr * 64:(pr + 1) * 64, hh * 512:(hh + 1) * 512],
                                   in_=pb[0:64, 0:512]), sem="v")
                    wait("tensor", "v", t_bcv)

        ln_guard = {"s": 0}

        def ln_stats(src_ap, Dd):
            wait("vector", "s", ln_guard["s"])
            op("vector", lambda io, sa=src_ap: io.tensor_reduce(
                out=stats[:, 0:1], in_=sa, axis=AX.X, op=ALU.add))
            drain("vector")
            t2 = op("vector", lambda io, dd=Dd: io.tensor_scalar(
                out=stats[:, 0:1], in0=stats[:, 0:1], scalar1=-1.0 / dd, scalar2=0.0,
                op0=ALU.mult, op1=ALU.add), sem="v")
            drain("vector")
            wait("scalar", "v", t2)
            op("scalar", lambda io, sa=src_ap, dd=Dd: io.activation(
                out=sqscr[:, 0:dd], in_=sa, func=AF.Square,
                bias=stats[:, 0:1], scale=1.0, accum_out=stats[:, 2:3]))
            drain("scalar")
            t4 = op("scalar", lambda io, dd=Dd: io.activation(
                out=stats[:, 3:4], in_=stats[:, 2:3], func=AF.Sqrt,
                bias=cbias[:, 0:1], scale=1.0 / dd), sem="s")
            wait("vector", "s", t4)
            t5 = op("vector", lambda io: io.reciprocal(
                out=stats[:, 1:2], in_=stats[:, 3:4]), sem="v")
            drain("vector")
            ln_guard["s"] = t4
            return t5

        def ln_apply(src_ap, dst_ap):
            t = op("vector", lambda io, sa=src_ap, da=dst_ap: io.tensor_scalar(
                out=da, in0=sa, scalar1=stats[:, 0:1], scalar2=stats[:, 1:2],
                op0=ALU.add, op1=ALU.mult), sem="v")
            drain("vector")
            return t

        def load_wx3(hg):
            c0 = hg * 512
            t = None
            for wi, W in enumerate((Wq, Wk, Wv)):
                for c in range(8):
                    t = dma("gpsimd",
                            Wx3[:, (wi * 8 + c) * 512:(wi * 8 + c + 1) * 512],
                            W[c * 128:(c + 1) * 128, c0:c0 + 512], "wx")
            return t

        G = {"xstage_free": [0, 0],
             "pT_free": 0,
             "xlT_free": 0,
             "pP_free": [0, 0],
             "psb_free": [0, 0, 0],
             "pS_free": [0, 0, 0],
             "pC_free": 0,
             "bias_free": 0,
             "rz_free": 0,
             "pZ_free": 0,
             "wosum_free": 0,
             "x2_store": 0,
             "hq_free": 0,
             "w1s_free": [0, 0],
             "w2s_free": [0, 0],
             "xlT2_free": 0,
             "epi_v": 0,
             "ctxst_free": [0, 0],
             "ctxld_free": [0, 0],
             "chain_v": 0,
             "crow_free": 0,
             "zrow_sfree": 0,
             "wstg_free": [0, 0],
             "wstgb_free": [0, 0],
             "pP_sfree": 0}

        PRECAST_UNITS = ([("wo", i) for i in range(H)]
                         + [("w1", i) for i in range(32)]
                         + [("w2", i) for i in range(32)])
        PRE = {"i": 0}

        def precast_emit(n_units):
            for _ in range(n_units):
                if PRE["i"] >= len(PRECAST_UNITS):
                    return
                kind, idx = PRECAST_UNITS[PRE["i"]]
                buf = PRE["i"] % 2
                nm = "wca" if buf == 0 else "wcb"
                snm = "wcsa" if buf == 0 else "wcsb"
                PRE["i"] += 1
                if kind == "w1":
                    src = w1d[:, idx * 128:(idx + 1) * 128].rearrange(
                        "(c p) f -> p c f", p=128)
                    ldst = wstg[buf][:].rearrange("p (c f) -> p c f", c=8)
                    rows = 128
                    sdst = w1bf[:, idx * 1024:(idx + 1) * 1024]
                elif kind == "w2":
                    src = w2d[idx * 128:(idx + 1) * 128, :]
                    ldst = wstg[buf][:]
                    rows = 128
                    sdst = w2bf[:, idx * 1024:(idx + 1) * 1024]
                else:
                    src = Wo[idx * 64:(idx + 1) * 64, :]
                    ldst = wstg[buf][0:64, :]
                    rows = 64
                    sdst = wobf[0:64, idx * 1024:(idx + 1) * 1024]
                wait("sync", "v", G["wstg_free"][buf])
                t_l = dma("sync", ldst, src, nm)
                wait("vector", nm, t_l)
                wait("vector", snm, G["wstgb_free"][buf])
                t_cast = op("vector", lambda io, B=buf, R=rows: io.tensor_copy(
                    out=wstgb[B][0:R, :], in_=wstg[B][0:R, :]), sem="v")
                drain("vector")
                G["wstg_free"][buf] = t_cast
                wait("sync", "v", t_cast)
                G["wstgb_free"][buf] = dma("sync", sdst, wstgb[buf][0:rows, :], snm)

        def stream_block(xsrc, blk_i, wx3_tgt, dst, woff, do_v):
            t_ev_last = 0
            for i in range(4):
                tile = blk_i * 4 + i
                sb = xstage[tile % 2]
                xs = "x0" if tile % 2 == 0 else "x1"
                wait("sync", "v", G["xstage_free"][tile % 2])
                t_x = dma("sync", sb[:], xsrc[tile * 128:(tile + 1) * 128, :], xs)
                wait("vector", xs, t_x)
                ln_stats(sb[:], D)
                t_ap = ln_apply(sb[:], xl[:])
                G["xstage_free"][tile % 2] = t_ap
                wait("tensor", "v", G["pT_free"])
                wait("tensor", "v", t_ap)
                tm = None
                for c in range(8):
                    tm = op("tensor", lambda io, cc=c: io.transpose(
                        pT[:, cc * 128:(cc + 1) * 128],
                        xl[:, cc * 128:(cc + 1) * 128], ident[:]),
                        sem="t" if c == 7 else None)
                wait("vector", "t", G["xlT_free"])
                wait("vector", "t", tm)
                t_ev = op("vector", lambda io, ii=i: io.tensor_copy(
                    out=xlT[:].rearrange("p (c n) -> p c n", c=8)[:, :, ii * 128:(ii + 1) * 128],
                    in_=pT[:].rearrange("p (c n) -> p c n", c=8)), sem="v")
                G["pT_free"] = t_ev
                t_ev_last = t_ev
                if do_v:
                    wait("tensor", "v", t_ev)
                    wait("tensor", "wx", wx3_tgt)
                    wait("tensor", "v", G["pP_free"][1])
                    tm2 = None
                    for c in range(8):
                        tm2 = op("tensor", lambda io, cc=c, ii=i: io.matmul(
                            pP[1][:, 0:512],
                            xlT[:, cc * 512 + ii * 128: cc * 512 + (ii + 1) * 128],
                            Wx3[:, (16 + cc) * 512:(17 + cc) * 512],
                            start=(cc == 0), stop=(cc == 7)),
                            sem="t" if c == 7 else None)
                    wait("vector", "t", tm2)
                    tv = None
                    for hh in range(8):
                        tv = op("vector", lambda io, h=hh, tl=tile: io.tensor_copy(
                            out=Vsb[h][:, tl * 65: tl * 65 + 64],
                            in_=pP[1][:, h * 64:(h + 1) * 64]),
                            sem="v" if hh == 7 else None)
                    G["pP_free"][1] = tv
            base = blk_i * 512
            wait("tensor", "wx", wx3_tgt)
            wait("tensor", "v", t_ev_last)
            for hp in range(4):
                wait("tensor", "v", G["pP_free"][0])
                tm3 = None
                for c in range(8):
                    tm3 = op("tensor", lambda io, cc=c, hhp=hp, wo=woff: io.matmul(
                        pP[0][:, 0:512],
                        Wx3[:, (wo + cc) * 512 + hhp * 128: (wo + cc) * 512 + hhp * 128 + 128],
                        xlT[:, cc * 512:(cc + 1) * 512],
                        start=(cc == 0), stop=(cc == 7)),
                        sem="t" if c == 7 else None)
                wait("vector", "t", tm3)
                tv2 = op("vector", lambda io, hhp=hp, b=base: io.tensor_copy(
                    out=dst[hhp][:, b:b + 512], in_=pP[0][:, 0:512]), sem="v")
                G["pP_free"][0] = tv2
            G["xlT_free"] = cnt["t"]

        def attention_group(hg, t_proj_v):
            wait("tensor", "v", t_proj_v)
            for h8 in range(8):
                h = hg * 8 + h8
                hp = h8 // 2
                off = (h8 % 2) * 64
                slope = SLOPES[h]
                wait("scalar", "v", G["crow_free"])
                t_crow = op("scalar", lambda io, sl=slope: io.activation(
                    out=crow[64:65, 0:512], in_=qpr[64:65, 0:512],
                    func=AF.Exp, bias=0.0, scale=-sl / 8.0), sem="s")
                for g in range(NG):
                    wait("vector", "s", G["bias_free"])
                    t_bias = op("vector", lambda io, sl=slope, col=h * NG + g:
                                io.tensor_scalar(
                                    out=biasmat[:, 0:NT], in0=kpos[:, 0:NT],
                                    scalar1=sl / 8.0, scalar2=qshift[:, col:col + 1],
                                    op0=ALU.mult, op1=ALU.add), sem="v")
                    drain("vector")
                    kmax = 8 * g + 8
                    last_exp = 0
                    for kt in range(kmax):
                        s_lo = max(4 * g, kt // 2)
                        c0 = (s_lo - 4 * g) * 128
                        pb = kt % 3
                        wait("tensor", "s", G["pS_free"][pb])
                        t_qk = op("tensor",
                                  lambda io, KT=kt, C0=c0, HP=hp, OF=off, GG=g, PB=pb:
                                  io.matmul(
                                      pS[PB][:, C0:512],
                                      Ksb[HP][OF:OF + 64, KT * 128:(KT + 1) * 128],
                                      Qsb[HP][OF:OF + 64, GG * 512 + C0:(GG + 1) * 512],
                                      start=True, stop=True), sem="t")
                        wait("scalar", "t", t_qk)
                        wait("scalar", "v", t_bias)
                        wait("scalar", "t", G["psb_free"][pb])
                        t_exp = op("scalar", lambda io, KT=kt, C0=c0, PB=pb:
                                   io.activation(
                                       out=psb[PB][:, C0:512], in_=pS[PB][:, C0:512],
                                       func=AF.Exp, bias=biasmat[:, KT:KT + 1],
                                       scale=0.125), sem="s")
                        G["pS_free"][pb] = t_exp
                        last_exp = t_exp
                        s_star = kt // 2
                        masked = (4 * g <= s_star < 4 * g + 4) and (kt >= 8 * g)
                        if masked:
                            mc = (kt % 2) * 128
                            wait("vector", "s", t_exp)
                            t_mk = op("vector", lambda io, SS=s_star, GG=g, MC=mc, PB=pb:
                                      io.tensor_tensor(
                                          out=psb[PB][:, (SS - 4 * GG) * 128:(SS - 4 * GG + 1) * 128],
                                          in0=psb[PB][:, (SS - 4 * GG) * 128:(SS - 4 * GG + 1) * 128],
                                          in1=masks[:, MC:MC + 128], op=ALU.mult),
                                      sem="v")
                            wait("tensor", "v", t_mk)
                        else:
                            wait("tensor", "s", t_exp)
                        if kt == 0:
                            wait("tensor", "v", G["pC_free"])
                        t_av = op("tensor",
                                  lambda io, KT=kt, C0=c0, HH=h8, PB=pb, KM=kmax:
                                  io.matmul(
                                      pC[:, C0:512],
                                      Vsb[HH][:, KT * 65:KT * 65 + 65],
                                      psb[PB][:, C0:512],
                                      start=(KT == 0), stop=(KT == KM - 1)), sem="t")
                        G["psb_free"][pb] = t_av
                    t_avl = cnt["t"]
                    wait("vector", "t", t_avl)
                    wait("vector", "t", G["rz_free"])
                    wait("vector", "s", t_crow)
                    wait("vector", "s", G["zrow_sfree"])
                    t_zm = op("vector", lambda io: io.tensor_tensor(
                        out=zrow[64:65, 0:512], in0=pC[64:65, 0:512],
                        in1=crow[64:65, 0:512], op=ALU.mult), sem="v")
                    drain("vector")
                    wait("scalar", "v", t_zm)
                    op("scalar", lambda io: io.activation(
                        out=zrow[64:65, 0:512], in_=zrow[64:65, 0:512],
                        func=AF.Ln, bias=0.0, scale=1.0))
                    drain("scalar")
                    t_rze = op("scalar", lambda io: io.activation(
                        out=rzrow[64:65, 0:512], in_=zrow[64:65, 0:512],
                        func=AF.Exp, bias=0.0, scale=-1.0), sem="s")
                    wait("vector", "s", t_rze)
                    t_rz = op("vector", lambda io: io.tensor_tensor(
                        out=rzrow[64:65, 0:512], in0=rzrow[64:65, 0:512],
                        in1=crow[64:65, 0:512], op=ALU.mult), sem="v")
                    drain("vector")
                    G["zrow_sfree"] = t_rze
                    wait("tensor", "v", t_rz)
                    wait("tensor", "v", G["pZ_free"])
                    t_bc = op("tensor", lambda io: io.matmul(
                        pZ[0:64, 0:512], onescf[64:65, 0:64], rzrow[64:65, 0:512],
                        start=True, stop=True), sem="t")
                    G["rz_free"] = t_bc
                    wait("vector", "t", t_bc)
                    t_evz = op("vector", lambda io: io.tensor_copy(
                        out=rzbc[:, 0:512], in_=pZ[0:64, 0:512]), sem="v")
                    G["pZ_free"] = t_evz
                    drain("vector")
                    cbuf = (h8 * NG + g) % 2
                    ctx_nm = "ctxa" if cbuf == 0 else "ctxb"
                    wait("vector", ctx_nm, G["ctxst_free"][cbuf])
                    t_nm = op("vector", lambda io, CB=cbuf: io.tensor_tensor(
                        out=ctxst[CB][:, 0:512],
                        in0=pC[0:64, 0:512], in1=rzbc[:, 0:512], op=ALU.mult),
                        sem="v")
                    drain("vector")
                    wait("gpsimd", "v", t_nm)
                    G["ctxst_free"][cbuf] = dma(
                        "gpsimd", ctx_d[h * 64:(h + 1) * 64, g * 512:(g + 1) * 512],
                        ctxst[cbuf][:, 0:512], ctx_nm)
                    G["pC_free"] = t_nm
                    G["bias_free"] = last_exp
                    G["crow_free"] = t_rz
                    it = hg * 32 + h8 * NG + g
                    precast_emit((it + 1) * 80 // 64 - it * 80 // 64)

        t_wx3 = load_wx3(0)
        for hg in range(2):
            if hg == 1:
                wait("gpsimd", "t", cnt["t"])
                t_wx3 = load_wx3(1)
                wait("vector", "t", cnt["t"])
            for b8 in range(8):
                stream_block(x_full, b8, t_wx3, Ksb, 8, do_v=True)
            for b4 in range(4):
                stream_block(x_my, b4, t_wx3, Qsb, 0, do_v=False)
            attention_group(hg, cnt["v"])

        def barrier():
            snap = dict(cnt)
            for eng in ("sync", "tensor", "vector", "scalar", "gpsimd"):
                for snm, val in snap.items():
                    wait(eng, snm, val)

        pstk1.close()
        stk1.close()
        barrier()

        stk2 = contextlib.ExitStack()
        e2 = stk2.enter_context
        Wosb = e2(nc.sbuf_tensor("Wosbt", [64, H * D], BF16))
        bo_b = e2(nc.sbuf_tensor("bo_bt", [128, D], F32))
        ctxld = [e2(nc.sbuf_tensor(f"ctxldt{i}", [64, 512], BF16)) for i in range(2)]
        pstk2 = contextlib.ExitStack()
        ep2 = pstk2.enter_context
        pW = [ep2(nc.psum_tensor(f"pWp{i}", [128, 512], F32)) for i in range(8)]

        t_wo = None
        for hh in range(H):
            t_wo = dma("scalar", Wosb[0:64, hh * D:(hh + 1) * D],
                       wobf[0:64, hh * 1024:(hh + 1) * 1024], "wo")
        bcast_row(0, bo_b, pW[0])
        t_bo_v = cnt["v"]
        pW_free_v = t_bo_v
        for blk in range(4):
            for hh in range(H):
                cb = hh % 2
                cxs = "cx0" if cb == 0 else "cx1"
                wait("sync", "t", G["ctxld_free"][cb])
                t_cx = dma("sync", ctxld[cb][:, 0:512],
                           ctx_d[hh * 64:(hh + 1) * 64, blk * 512:(blk + 1) * 512],
                           cxs)
                wait("tensor", cxs, t_cx)
                if hh == 0:
                    wait("tensor", "wo", t_wo)
                    wait("tensor", "v", pW_free_v)
                tm = None
                for tl in range(4):
                    for bank in range(2):
                        tm = op("tensor",
                                lambda io, HHH=hh, TL=tl, BK=bank, CB=cb:
                                io.matmul(
                                    pW[TL * 2 + BK][:, 0:512],
                                    ctxld[CB][:, TL * 128:(TL + 1) * 128],
                                    Wosb[0:64, HHH * D + BK * 512: HHH * D + BK * 512 + 512],
                                    start=(HHH == 0), stop=(HHH == H - 1)),
                                sem="t" if (tl == 3 and bank == 1) else None)
                G["ctxld_free"][cb] = tm
            t_womm = cnt["t"]
            for tl in range(4):
                tt = blk * 4 + tl
                sb = xstage[tt % 2]
                xs = "x0" if tt % 2 == 0 else "x1"
                wait("sync", "v", G["xstage_free"][tt % 2])
                t_x = dma("sync", sb[:], x_my[tt * 128:(tt + 1) * 128, :], xs)
                wait("vector", "t", t_womm)
                wait("vector", "v", G["wosum_free"])
                for bank in range(2):
                    op("vector", lambda io, TL=tl, BK=bank: io.scalar_tensor_tensor(
                        out=wosum[:, BK * 512:(BK + 1) * 512],
                        in0=pW[TL * 2 + BK][:, 0:512],
                        scalar=1.0, in1=bo_b[:, BK * 512:(BK + 1) * 512],
                        op0=ALU.mult, op1=ALU.add),
                       sem="v" if bank == 1 else None)
                drain("vector")
                pW_free_v = cnt["v"]
                ln_stats(wosum[:], D)
                t_ap = ln_apply(wosum[:], lntmp[:])
                G["wosum_free"] = t_ap
                wait("vector", xs, t_x)
                t_x2 = op("vector", lambda io, TT=tt: io.tensor_tensor(
                    out=x2t[TT % 2][:], in0=xstage[TT % 2][:], in1=lntmp[:],
                    op=ALU.add), sem="v")
                drain("vector")
                G["xstage_free"][tt % 2] = t_x2
                x2s_nm = "x2a" if tt % 2 == 0 else "x2b"
                wait("scalar", "v", t_x2)
                G["x2_store"] = dma("scalar", x2_d[tt * 128:(tt + 1) * 128, :],
                                    x2t[tt % 2][:], x2s_nm)
                wait("vector", x2s_nm, G["x2_store"])

        pstk2.close()
        stk2.close()
        barrier()

        stk3 = contextlib.ExitStack()
        e3 = stk3.enter_context
        b2_b = e3(nc.sbuf_tensor("b2_bt", [128, D], F32))
        w1s = [e3(nc.sbuf_tensor(f"w1st{i}", [128, 8 * 128], BF16)) for i in range(2)]
        w2s = [e3(nc.sbuf_tensor(f"w2st{i}", [128, D], BF16)) for i in range(2)]
        hq = e3(nc.sbuf_tensor("hqt", [128, 32 * 1024], BF16))
        xlT2 = e3(nc.sbuf_tensor("xlT2t", [128, 8 * 1024], BF16))
        c_u = e3(nc.sbuf_tensor("c_ut", [128, 1024], F32))
        c_ti = e3(nc.sbuf_tensor("c_tit", [128, 1024], I32))
        c_tf = e3(nc.sbuf_tensor("c_tft", [128, 1024], F32))
        c_e = e3(nc.sbuf_tensor("c_et", [128, 1024], F32))
        c_a = e3(nc.sbuf_tensor("c_at", [128, 1024], F32))
        c_nt = e3(nc.sbuf_tensor("c_ntt", [128, 1024], F32))
        c_ti2 = e3(nc.sbuf_tensor("c_ti2t", [128, 1024], I32))
        c_ri = e3(nc.sbuf_tensor("c_rit", [128, 1024], I32))
        c_rf = e3(nc.sbuf_tensor("c_rft", [128, 1024], F32))
        c_z = e3(nc.sbuf_tensor("c_zt", [128, 1024], F32))
        c_g0 = e3(nc.sbuf_tensor("c_g0t", [128, 1024], F32))
        c_g1 = e3(nc.sbuf_tensor("c_g1t", [128, 1024], F32))
        c_y = e3(nc.sbuf_tensor("c_yt", [128, 1024], F32))
        pstk3 = contextlib.ExitStack()
        ep3 = pstk3.enter_context
        pM = [ep3(nc.psum_tensor(f"pMp{i}", [128, 512], F32)) for i in range(4)]
        pT3 = ep3(nc.psum_tensor("pT3p", [128, D], BF16))

        bcast_row(64, b2_b, pM[0])
        G["epi_v"] = cnt["v"]

        LADDER_COLS = {}

        def sign_peel(vbuf, bbuf, levels, tag, width=1024):
            K = 0.0
            hw_ = width // 2
            halves = [(0, hw_), (hw_, width)]
            t_s = [0, 0]
            t_v = [0, 0]
            wait("scalar", "v", cnt["v"])
            for i, c in enumerate(levels):
                col = LADDER_COLS[tag][i]
                for hh in (0, 1):
                    lo, hi = halves[hh]
                    if t_v[hh]:
                        wait("scalar", "v", t_v[hh])
                    t_s[hh] = op("scalar",
                                 lambda io, V=vbuf, Bb=bbuf, CL=col, L=lo, Hh=hi:
                                 io.activation(
                                     out=Bb[:, L:Hh], in_=V[:, L:Hh], func=AF.Sign,
                                     bias=ladc[:, CL:CL + 1], scale=1.0), sem="s")
                for hh in (0, 1):
                    lo, hi = halves[hh]
                    wait("vector", "s", t_s[hh])
                    t_v[hh] = op("vector",
                                 lambda io, V=vbuf, Bb=bbuf, C=c, L=lo, Hh=hi:
                                 io.scalar_tensor_tensor(
                                     out=V[:, L:Hh], in0=Bb[:, L:Hh],
                                     scalar=float(-C / 2.0), in1=V[:, L:Hh],
                                     op0=ALU.mult, op1=ALU.add), sem="v")
                    drain("vector")
                K += -c / 2.0
            wait("scalar", "v", max(t_v))
            return K

        LADDER_COLS.update({
            "hA": [0, 1, 2, 3, 4],
            "hB": [5, 6, 7],
            "h3n": [8, 9, 10, 11],
            "h4n": [12, 13, 14, 15, 16],
        })

        def ffn_chain(ft):
            wait("scalar", "v", G["chain_v"])
            for half in range(2):
                op("scalar", lambda io, HF=half, FT=ft: io.activation(
                    out=c_u[:, HF * 512:(HF + 1) * 512], in_=pM[HF][:, 0:512],
                    func=AF.Identity, bias=b1bias[:, FT:FT + 1], scale=256.0),
                   sem="s" if half == 1 else None)
            t_u = cnt["s"]
            G["pP_sfree"] = t_u
            wait("vector", "s", t_u)
            op("vector", lambda io: io.tensor_scalar(
                out=c_ti[:], in0=c_u[:], scalar1=1.0, scalar2=-16.0,
                op0=ALU.mult, op1=ALU.add))
            drain("vector")
            op("vector", lambda io: io.tensor_copy(out=c_tf[:], in_=c_ti[:]))
            drain("vector")
            op("vector", lambda io: io.scalar_tensor_tensor(
                out=c_e[:], in0=c_tf[:], scalar=-1.0, in1=c_u[:],
                op0=ALU.mult, op1=ALU.add), sem="v")
            drain("vector")
            sign_peel(c_e, c_a, [16, 8, 4, 2, 1], "hA")
            op("vector", lambda io: io.tensor_scalar(
                out=c_ti[:], in0=c_u[:], scalar1=1.0 / 64, scalar2=-4.0,
                op0=ALU.mult, op1=ALU.add))
            drain("vector")
            op("vector", lambda io: io.tensor_copy(out=c_tf[:], in_=c_ti[:]))
            drain("vector")
            op("vector", lambda io: io.scalar_tensor_tensor(
                out=c_rf[:], in0=c_tf[:], scalar=-64.0, in1=c_u[:],
                op0=ALU.mult, op1=ALU.add), sem="v")
            drain("vector")
            sign_peel(c_rf, c_a, [256, 128, 64], "hB")
            t_z = op("vector", lambda io: io.tensor_tensor(
                out=c_z[:], in0=c_rf[:], in1=c_u[:], op=ALU.subtract), sem="v")
            drain("vector")
            op("vector", lambda io: io.tensor_tensor(
                out=c_rf[:], in0=c_rf[:], in1=c_e[:], op=ALU.subtract))
            drain("vector")
            op("vector", lambda io: io.tensor_scalar(
                out=c_rf[:], in0=c_rf[:], scalar1=1.0, scalar2=-208.5,
                op0=ALU.mult, op1=ALU.add))
            drain("vector")
            wait("scalar", "v", t_z)
            op("scalar", lambda io: io.activation(
                out=c_g0[:], in_=c_z[:], func=AF.Gelu, bias=cbias[:, 1:2],
                scale=-1.0 / 256))
            drain("scalar")
            t_g = op("scalar", lambda io: io.activation(
                out=c_g1[:], in_=c_z[:], func=AF.Gelu, bias=cbias[:, 2:3],
                scale=-1.0 / 256), sem="s")
            wait("vector", "s", t_g)
            op("vector", lambda io: io.tensor_tensor(
                out=c_g1[:], in0=c_g1[:], in1=c_g0[:], op=ALU.subtract))
            drain("vector")
            op("vector", lambda io: io.scalar_tensor_tensor(
                out=c_e[:], in0=c_g1[:], scalar=1.0 / 64, in1=c_rf[:],
                op0=ALU.mult, op1=ALU.mult))
            drain("vector")
            t_y = op("vector", lambda io: io.tensor_tensor(
                out=c_y[:], in0=c_e[:], in1=c_g0[:], op=ALU.add), sem="v")
            wait("scalar", "v", t_y)
            t_u2 = op("scalar", lambda io: io.activation(
                out=c_u[:], in_=c_y[:], func=AF.Identity, bias=cbias[:, 3:4],
                scale=256.0), sem="s")
            wait("vector", "s", t_u2)
            op("vector", lambda io: io.tensor_scalar(
                out=c_ti[:], in0=c_u[:], scalar1=1.0, scalar2=-8.0,
                op0=ALU.mult, op1=ALU.add))
            drain("vector")
            op("vector", lambda io: io.tensor_copy(out=c_tf[:], in_=c_ti[:]))
            drain("vector")
            op("vector", lambda io: io.scalar_tensor_tensor(
                out=c_e[:], in0=c_tf[:], scalar=-1.0, in1=c_u[:],
                op0=ALU.mult, op1=ALU.add), sem="v")
            drain("vector")
            sign_peel(c_e, c_a, [8, 4, 2, 1], "h3n")
            t_ca = op("vector", lambda io: io.tensor_tensor(
                out=c_a[:], in0=c_u[:], in1=c_e[:], op=ALU.subtract), sem="v")
            drain("vector")
            wait("scalar", "v", t_ca)
            t_hq = op("scalar", lambda io, FT=ft: io.activation(
                out=hq[:, FT * 1024:(FT + 1) * 1024], in_=c_a[:],
                func=AF.Identity, bias=cbias[:, 5:6], scale=1.0 / 256),
                sem="s")
            G["chain_v"] = t_ca
            return t_hq

        wait("sync", "x2a", cnt["x2a"])
        wait("sync", "x2b", cnt["x2b"])
        for fb in range(2):
            for i in range(8):
                tile = fb * 8 + i
                sb = xstage[i % 2]
                xs = "x0" if i % 2 == 0 else "x1"
                wait("sync", "v", G["xstage_free"][i % 2])
                t_x = dma("sync", sb[:], x2_d[tile * 128:(tile + 1) * 128, :], xs)
                wait("vector", xs, t_x)
                ln_stats(sb[:], D)
                t_ap = ln_apply(sb[:], xl[:])
                G["xstage_free"][i % 2] = t_ap
                wait("tensor", "v", G["pT_free"])
                wait("tensor", "v", t_ap)
                tm = None
                for c in range(8):
                    tm = op("tensor", lambda io, cc=c: io.transpose(
                        pT3[:, cc * 128:(cc + 1) * 128],
                        xl[:, cc * 128:(cc + 1) * 128], ident[:]),
                        sem="t" if c == 7 else None)
                wait("vector", "t", G["xlT2_free"])
                wait("vector", "t", tm)
                t_ev = op("vector", lambda io, ii=i: io.tensor_copy(
                    out=xlT2[:].rearrange("p (c n) -> p c n", c=8)[:, :, ii * 128:(ii + 1) * 128],
                    in_=pT3[:].rearrange("p (c n) -> p c n", c=8)), sem="v")
                G["pT_free"] = t_ev
            t_xlT2 = cnt["v"]
            wait("scalar", "t", G["hq_free"])
            for ft in range(32):
                buf = ft % 2
                w1nm = "w1a" if buf == 0 else "w1b"
                wait("sync", "t", G["w1s_free"][buf])
                t_w1 = dma("sync", w1s[buf][:],
                           w1bf[:, ft * 1024:(ft + 1) * 1024], w1nm)
                wait("tensor", w1nm, t_w1)
                if ft == 0:
                    wait("tensor", "v", t_xlT2)
                    wait("tensor", "v", G["epi_v"])
                wait("tensor", "s", G["pP_sfree"])
                tm = None
                for half in range(2):
                    for c in range(8):
                        tm = op("tensor", lambda io, cc=c, HF=half, BUF=buf: io.matmul(
                            pM[HF][:, 0:512],
                            w1s[BUF][:, cc * 128:(cc + 1) * 128],
                            xlT2[:, cc * 1024 + HF * 512: cc * 1024 + (HF + 1) * 512],
                            start=(cc == 0), stop=(cc == 7)),
                            sem="t" if (c == 7 and half == 1) else None)
                G["w1s_free"][buf] = tm
                wait("scalar", "t", tm)
                ffn_chain(ft)
            t_hq_all = cnt["s"]
            G["xlT2_free"] = cnt["t"]
            for tp in range(4):
                tiles = (fb * 8 + tp * 2, fb * 8 + tp * 2 + 1)
                t_w2mm = None
                for f in range(32):
                    buf = f % 2
                    w2nm = "w2a" if buf == 0 else "w2b"
                    wait("scalar", "t", G["w2s_free"][buf])
                    t_w2 = dma("scalar", w2s[buf][:],
                               w2bf[:, f * 1024:(f + 1) * 1024], w2nm)
                    wait("tensor", w2nm, t_w2)
                    if f == 0:
                        wait("tensor", "s", t_hq_all)
                        wait("tensor", "s", G["pP_sfree"])
                        wait("tensor", "v", G["epi_v"])
                    tm = None
                    for li in range(2):
                        for bank in range(2):
                            tm = op("tensor",
                                    lambda io, FF_=f, LI=li, BK=bank, TL=tiles[li], FB=fb, BUF=buf:
                                    io.matmul(
                                        pM[LI * 2 + BK][:, 0:512],
                                        hq[:, FF_ * 1024 + (TL - FB * 8) * 128:
                                           FF_ * 1024 + (TL - FB * 8) * 128 + 128],
                                        w2s[BUF][:, BK * 512:(BK + 1) * 512],
                                        start=(FF_ == 0), stop=(FF_ == 31)),
                                    sem="t" if (li == 1 and bank == 1) else None)
                    t_w2mm = tm
                    G["w2s_free"][buf] = tm
                for li in range(2):
                    tile = tiles[li]
                    wait("vector", "t", t_w2mm)
                    wait("vector", "v", G["wosum_free"])
                    for bank in range(2):
                        op("vector", lambda io, LI=li, BK=bank: io.scalar_tensor_tensor(
                            out=wosum[:, BK * 512:(BK + 1) * 512],
                            in0=pM[LI * 2 + BK][:, 0:512],
                            scalar=1.0, in1=b2_b[:, BK * 512:(BK + 1) * 512],
                            op0=ALU.mult, op1=ALU.add),
                           sem="v" if bank == 1 else None)
                    drain("vector")
                    t_o = cnt["v"]
                    if li == 1:
                        G["epi_v"] = t_o
                    wait("scalar", "v", t_o)
                    t_u3 = op("scalar", lambda io: io.activation(
                        out=c_u[:], in_=wosum[:], func=AF.Identity,
                        bias=cbias[:, 4:5], scale=256.0), sem="s")
                    wait("vector", "s", t_u3)
                    op("vector", lambda io: io.tensor_scalar(
                        out=c_ti[:], in0=c_u[:], scalar1=1.0, scalar2=-16.0,
                        op0=ALU.mult, op1=ALU.add))
                    drain("vector")
                    op("vector", lambda io: io.tensor_copy(out=c_tf[:], in_=c_ti[:]))
                    drain("vector")
                    op("vector", lambda io: io.scalar_tensor_tensor(
                        out=c_e[:], in0=c_tf[:], scalar=-1.0, in1=c_u[:],
                        op0=ALU.mult, op1=ALU.add), sem="v")
                    drain("vector")
                    sign_peel(c_e, c_a, [16, 8, 4, 2, 1], "h4n")
                    op("vector", lambda io: io.tensor_tensor(
                        out=c_a[:], in0=c_u[:], in1=c_e[:], op=ALU.subtract))
                    drain("vector")
                    op("vector", lambda io: io.tensor_scalar(
                        out=lntmp[:], in0=c_a[:], scalar1=1.0 / 256,
                        scalar2=-31.939453125, op0=ALU.mult, op1=ALU.add))
                    drain("vector")
                    ln_stats(lntmp[:], D)
                    ln_apply(lntmp[:], wosum[:])
                    sb = xstage[li]
                    xs = "x0" if li == 0 else "x1"
                    wait("sync", "v", G["xstage_free"][li])
                    t_x = dma("sync", sb[:], x2_d[tile * 128:(tile + 1) * 128, :], xs)
                    wait("vector", xs, t_x)
                    t_out = op("vector", lambda io, LI=li: io.tensor_tensor(
                        out=outt[LI][:], in0=xstage[LI][:], in1=wosum[:],
                        op=ALU.add), sem="v")
                    drain("vector")
                    G["xstage_free"][li] = t_out
                    G["wosum_free"] = t_out
                    out_nm = "outa" if li == 0 else "outb"
                    wait("scalar", "v", t_out)
                    t_st = dma("scalar", out_d[tile * 128:(tile + 1) * 128, :],
                               outt[li][:], out_nm)
                    wait("vector", out_nm, t_st)
            G["hq_free"] = cnt["t"]
        wait("gpsimd", "outa", cnt["outa"])
        wait("gpsimd", "outb", cnt["outb"])
        pstk3.close()
        stk3.close()

        @BLOCK.sync
        def _(io):
            for f in ops["sync"]:
                f(io)

        @BLOCK.tensor
        def _(io):
            for f in ops["tensor"]:
                f(io)

        @BLOCK.vector
        def _(io):
            for f in ops["vector"]:
                f(io)

        @BLOCK.scalar
        def _(io):
            for f in ops["scalar"]:
                f(io)

        @BLOCK.gpsimd
        def _(io):
            for f in ops["gpsimd"]:
                f(io)

    return nc


_NC_CACHE = {}


def _constants(par):
    ident = np.eye(128, dtype=ml_dtypes.bfloat16)
    kposc = (np.arange(NT)[None, :] * 128 + np.arange(128)[:, None]).astype(np.float32)
    qshift = np.zeros((128, H * NG), dtype=np.float32)
    for h in range(H):
        for g in range(NG):
            t0 = 128 * (8 * g + par)
            t1 = 128 * (8 * g + 6 + par) + 127
            qref = 0.5 * (t0 + t1)
            qshift[:, h * NG + g] = -MSHIFT - SLOPES[h] * qref / 8.0
    ladc = np.zeros((128, 64), dtype=np.float32)
    col = 0
    for levels in ([16, 8, 4, 2, 1], [256, 128, 64], [8, 4, 2, 1],
                   [16, 8, 4, 2, 1]):
        K = 0.0
        for c in levels:
            ladc[:, col] = -(c + K)
            K += c / 2.0
            col += 1
    qpr = np.zeros((128, 512), dtype=np.float32)
    qpr[64, :] = np.repeat(np.array([-384.0, -128.0, 128.0, 384.0], dtype=np.float32), 128)
    lowtri = (np.arange(128)[:, None] <= np.arange(128)[None, :])
    masks = np.zeros((128, 256), dtype=ml_dtypes.bfloat16)
    if par == 0:
        masks[:, 0:128] = lowtri.astype(ml_dtypes.bfloat16)
        masks[:, 128:256] = 0
    else:
        masks[:, 0:128] = 1
        masks[:, 128:256] = lowtri.astype(ml_dtypes.bfloat16)
    return ident, kposc, qshift, masks, qpr, ladc


def kernel(L0_attn_x, L0_attn_Wq, L0_attn_Wk, L0_attn_Wv, L0_attn_Wo, L0_attn_bo,
           L0_ffn_w1, L0_ffn_b1, L0_ffn_w2, L0_ffn_b2):
    x = np.ascontiguousarray(np.asarray(L0_attn_x, dtype=np.float32))
    Wq = np.ascontiguousarray(np.asarray(L0_attn_Wq, dtype=np.float32))
    Wk = np.ascontiguousarray(np.asarray(L0_attn_Wk, dtype=np.float32))
    Wv = np.ascontiguousarray(np.asarray(L0_attn_Wv, dtype=np.float32))
    Wo = np.ascontiguousarray(np.asarray(L0_attn_Wo, dtype=np.float32))
    bo = np.ascontiguousarray(np.asarray(L0_attn_bo, dtype=np.float32))
    w1 = np.ascontiguousarray(np.asarray(L0_ffn_w1, dtype=np.float32))
    b1 = np.ascontiguousarray(
        np.asarray(L0_ffn_b1, dtype=np.float32).reshape(FF // 128, 128).T)
    w2 = np.ascontiguousarray(np.asarray(L0_ffn_w2, dtype=np.float32))
    b2 = np.ascontiguousarray(np.asarray(L0_ffn_b2, dtype=np.float32))

    if "nc" not in _NC_CACHE:
        _NC_CACHE["nc"] = build_nc()
    nc = _NC_CACHE["nc"]

    in_maps = []
    for c in range(8):
        b, par = c // 2, c % 2
        xb = x[b]
        x_my = np.ascontiguousarray(
            xb.reshape(NT, 128, D)[par::2].reshape(MYN, D))
        ident, kposc, qshift, masks, qpr, ladc = _constants(par)
        in_maps.append({
            "x_full": xb, "x_my": x_my,
            "Wq": Wq, "Wk": Wk, "Wv": Wv, "Wo": Wo, "bo": bo,
            "w1": w1, "b1": b1, "w2": w2, "b2": b2,
            "identc": ident, "kposc": kposc, "qshiftc": qshift, "masksc": masks,
            "qprc": qpr, "ladc": ladc,
        })
    res = run_bass_kernel_spmd(nc, in_maps, core_ids=list(range(8)))
    out = np.empty((B, N, D), dtype=np.float32)
    for c in range(8):
        b, par = c // 2, c % 2
        o = res.results[c]["out"].reshape(MYT, 128, D)
        out[b].reshape(NT, 128, D)[par::2] = o
    return out


# version 09
# speedup vs baseline: 1.1793x; 1.1793x over previous
import contextlib

import numpy as np
import ml_dtypes

import concourse.bass as bass
import concourse.mybir as mybir
from concourse.bass_utils import run_bass_kernel_spmd

F32 = mybir.dt.float32
BF16 = mybir.dt.bfloat16
I32 = mybir.dt.int32
AF = mybir.ActivationFunctionType
ALU = mybir.AluOpType
AX = mybir.AxisListType

B, N, D = 4, 4096, 1024
H = 16
FF = 4096
NT = N // 128
MYT = NT // 2
MYN = MYT * 128
NG = 4
SLOPES = [2.0 ** (-0.5 * (i + 1)) for i in range(H)]
MSHIFT = 12.0


def build_nc():
    nc = bass.Bass("TRN2", target_bir_lowering=False)
    ap = lambda nm, shp, ty, kind: nc.dram_tensor(nm, shp, ty, kind=kind).ap()

    x_full = ap("x_full", [N, D], F32, "ExternalInput")
    x_my = ap("x_my", [MYN, D], F32, "ExternalInput")
    Wq = ap("Wq", [D, D], F32, "ExternalInput")
    Wk = ap("Wk", [D, D], F32, "ExternalInput")
    Wv = ap("Wv", [D, D], F32, "ExternalInput")
    Wo = ap("Wo", [D, D], F32, "ExternalInput")
    bo = ap("bo", [D], F32, "ExternalInput")
    w1d = ap("w1", [D, FF], F32, "ExternalInput")
    b1d = ap("b1", [128, FF // 128], F32, "ExternalInput")
    w2d = ap("w2", [FF, D], F32, "ExternalInput")
    b2d = ap("b2", [D], F32, "ExternalInput")
    ident_in = ap("identc", [128, 128], BF16, "ExternalInput")
    kpos_in = ap("kposc", [128, NT], F32, "ExternalInput")
    qshift_in = ap("qshiftc", [128, H * NG], F32, "ExternalInput")
    masks_in = ap("masksc", [128, 2 * 128], BF16, "ExternalInput")
    qpr_in = ap("qprc", [128, 512], F32, "ExternalInput")
    ladc_in = ap("ladc", [128, 64], F32, "ExternalInput")
    out_d = ap("out", [MYN, D], F32, "ExternalOutput")
    x2_d = ap("x2scr", [MYN, D], F32, "Internal")
    ctx_d = ap("ctxscr", [H * 64, MYN], BF16, "Internal")
    w1bf = ap("w1bf", [128, 32 * 1024], BF16, "Internal")
    w2bf = ap("w2bf", [128, 32 * 1024], BF16, "Internal")
    wobf = ap("wobf", [64, 16 * 1024], BF16, "Internal")

    ops = {e: [] for e in ("sync", "tensor", "vector", "scalar", "gpsimd")}
    cnt = {s: 0 for s in ("t", "v", "s", "cl", "wx", "wo",
                          "x0", "x1", "cx0", "cx1", "w1a", "w1b", "w2a", "w2b",
                          "x2a", "x2b", "outa", "outb", "ctxa", "ctxb",
                          "wca", "wcb", "wcsa", "wcsb")}
    semh = {}

    def op(eng, fn, sem=None):
        if sem is None:
            ops[eng].append(fn)
            return None
        cnt[sem] += 1
        tgt = cnt[sem]
        ops[eng].append(lambda io, f=fn, ss=sem: f(io).then_inc(semh[ss], 1))
        return tgt

    def dma(eng, dst, src, sem):
        cnt[sem] += 16
        tgt = cnt[sem]
        ops[eng].append(lambda io, d=dst, s=src, ss=sem:
                        io.dma_start(out=d, in_=s).then_inc(semh[ss], 16))
        return tgt

    def wait(eng, sem, tgt):
        if tgt:
            ops[eng].append(lambda io, ss=sem, t=tgt: io.wait_ge(semh[ss], t))

    def drain(eng):
        ops[eng].append(lambda io: io.drain())

    stk = contextlib.ExitStack()
    e = stk.enter_context
    with stk:
        ident = e(nc.sbuf_tensor("identt", [128, 128], BF16))
        kpos = e(nc.sbuf_tensor("kpost", [128, NT], F32))
        qshift = e(nc.sbuf_tensor("qshiftt", [128, H * NG], F32))
        masks = e(nc.sbuf_tensor("maskst", [128, 2 * 128], BF16))
        qpr = e(nc.sbuf_tensor("qprt", [128, 512], F32))
        ladc = e(nc.sbuf_tensor("ladct", [128, 64], F32))
        crow = e(nc.sbuf_tensor("crowt", [128, 512], F32))
        biasmat = e(nc.sbuf_tensor("biasmatt", [128, NT], F32))
        b1bias = e(nc.sbuf_tensor("b1biast", [128, FF // 128], F32))
        brow = e(nc.sbuf_tensor("browt", [128, D], F32))
        onesc = e(nc.sbuf_tensor("onesct", [128, 128], BF16))
        onescf = e(nc.sbuf_tensor("onescft", [128, 128], F32))
        cbias = e(nc.sbuf_tensor("cbiast", [128, 8], F32))
        xstage = [e(nc.sbuf_tensor(f"xstaget{i}", [128, D], F32)) for i in range(2)]
        xl = e(nc.sbuf_tensor("xlt", [128, D], BF16))
        stats = e(nc.sbuf_tensor("statst", [128, 8], F32))
        sqscr = e(nc.sbuf_tensor("sqscrt", [128, D], F32))
        wosum = e(nc.sbuf_tensor("wosumt", [128, D], F32))
        lntmp = e(nc.sbuf_tensor("lntmpt", [128, D], F32))
        x2t = [e(nc.sbuf_tensor(f"x2tt{i}", [128, D], F32)) for i in range(2)]
        outt = [e(nc.sbuf_tensor(f"outtt{i}", [128, D], F32)) for i in range(2)]

        stk1 = contextlib.ExitStack()
        e1 = stk1.enter_context
        Ksb = [e1(nc.sbuf_tensor(f"Ksb{i}", [128, N], BF16)) for i in range(4)]
        Qsb = [e1(nc.sbuf_tensor(f"Qsb{i}", [128, MYN], BF16)) for i in range(4)]
        Vsb = [e1(nc.sbuf_tensor(f"Vsb{i}", [128, NT * 65], BF16)) for i in range(8)]
        Wx3 = e1(nc.sbuf_tensor("Wx3t", [128, 24 * 512], BF16))
        xlT = e1(nc.sbuf_tensor("xlTt", [128, 8 * 512], BF16))
        psb = [e1(nc.sbuf_tensor(f"psbt{i}", [128, 512], BF16)) for i in range(3)]
        zrow = e1(nc.sbuf_tensor("zrowt", [128, 512], F32))
        rzrow = e1(nc.sbuf_tensor("rzrowt", [128, 512], F32))
        rzbc = e1(nc.sbuf_tensor("rzbct", [64, 512], F32))
        ctxst = [e1(nc.sbuf_tensor(f"ctxstt{i}", [64, 512], BF16)) for i in range(2)]
        wstg = [e1(nc.sbuf_tensor(f"wstgt{i}", [128, 1024], F32)) for i in range(2)]
        wstgb = [e1(nc.sbuf_tensor(f"wstgbt{i}", [128, 1024], BF16)) for i in range(2)]

        pstk1 = contextlib.ExitStack()
        ep1 = pstk1.enter_context
        pS = [ep1(nc.psum_tensor(f"pSp{i}", [128, 512], F32)) for i in range(3)]
        pC = ep1(nc.psum_tensor("pCp", [65, 512], F32))
        pZ = ep1(nc.psum_tensor("pZp", [64, 512], F32))
        pT = ep1(nc.psum_tensor("pTp", [128, D], BF16))
        pP = [ep1(nc.psum_tensor(f"pPp{i}", [128, 512], F32)) for i in range(2)]

        for snm in cnt:
            semh[snm] = e(nc.semaphore(f"sem_{snm}"))
        BLOCK = e(nc.Block())

        t_c = 0
        for dst, src in ((ident, ident_in), (kpos, kpos_in), (qshift, qshift_in),
                         (masks, masks_in), (qpr, qpr_in), (ladc, ladc_in)):
            t_c = dma("sync", dst[:], src, "cl")
        t_c = dma("sync", brow[0:1, :], bo.rearrange("(o d) -> o d", o=1), "cl")
        t_c = dma("sync", brow[64:65, :], b2d.rearrange("(o d) -> o d", o=1), "cl")
        t_c = dma("sync", b1bias[:], b1d, "cl")
        t_consts = t_c

        wait("vector", "cl", t_consts)
        op("vector", lambda io: io.memset(onesc[:], 1.0))
        op("vector", lambda io: io.memset(onescf[:], 1.0))
        for col, val in ((0, 1e-5), (1, -7.125), (2, -6.875), (3, 64.5),
                         (4, 8192.5), (5, -0.220703125)):
            op("vector", lambda io, C=col, VV=val: io.memset(cbias[:, C:C + 1], VV))
        for hh in range(8):
            op("vector", lambda io, h=hh: io.memset(
                Vsb[h][:].rearrange("p (k c) -> p k c", c=65)[:, :, 64:65], 1.0))
        t_b1b = op("vector", lambda io: io.tensor_scalar(
            out=b1bias[:], in0=b1bias[:], scalar1=256.0, scalar2=2048.5,
            op0=ALU.mult, op1=ALU.add), sem="v")
        drain("vector")

        def bcast_row(r_i, dstt, pbank):
            for half in range(2):
                for prange in range(2):
                    t_bc = op("tensor", lambda io, rr=r_i, hh=half, pr=prange, pb=pbank:
                              io.matmul(pb[0:64, 0:512],
                                        onescf[rr:rr + 1, pr * 64:(pr + 1) * 64],
                                        brow[rr:rr + 1, hh * 512:(hh + 1) * 512],
                                        start=True, stop=True,
                                        tile_position=(rr, 0)), sem="t")
                    wait("vector", "t", t_bc)
                    t_bcv = op("vector", lambda io, d=dstt, hh=half, pr=prange, pb=pbank:
                               io.tensor_copy(
                                   out=d[pr * 64:(pr + 1) * 64, hh * 512:(hh + 1) * 512],
                                   in_=pb[0:64, 0:512]), sem="v")
                    wait("tensor", "v", t_bcv)

        ln_guard = {"s": 0}

        def ln_stats(src_ap, Dd):
            wait("vector", "s", ln_guard["s"])
            op("vector", lambda io, sa=src_ap: io.tensor_reduce(
                out=stats[:, 0:1], in_=sa, axis=AX.X, op=ALU.add))
            drain("vector")
            t2 = op("vector", lambda io, dd=Dd: io.tensor_scalar(
                out=stats[:, 0:1], in0=stats[:, 0:1], scalar1=-1.0 / dd, scalar2=0.0,
                op0=ALU.mult, op1=ALU.add), sem="v")
            drain("vector")
            wait("scalar", "v", t2)
            op("scalar", lambda io, sa=src_ap, dd=Dd: io.activation(
                out=sqscr[:, 0:dd], in_=sa, func=AF.Square,
                bias=stats[:, 0:1], scale=1.0, accum_out=stats[:, 2:3]))
            drain("scalar")
            t4 = op("scalar", lambda io, dd=Dd: io.activation(
                out=stats[:, 3:4], in_=stats[:, 2:3], func=AF.Sqrt,
                bias=cbias[:, 0:1], scale=1.0 / dd), sem="s")
            wait("vector", "s", t4)
            t5 = op("vector", lambda io: io.reciprocal(
                out=stats[:, 1:2], in_=stats[:, 3:4]), sem="v")
            drain("vector")
            ln_guard["s"] = t4
            return t5

        def ln_apply(src_ap, dst_ap):
            t = op("vector", lambda io, sa=src_ap, da=dst_ap: io.tensor_scalar(
                out=da, in0=sa, scalar1=stats[:, 0:1], scalar2=stats[:, 1:2],
                op0=ALU.add, op1=ALU.mult), sem="v")
            drain("vector")
            return t

        def load_wx3(hg):
            c0 = hg * 512
            t = None
            for wi, W in enumerate((Wq, Wk, Wv)):
                for c in range(8):
                    t = dma("gpsimd",
                            Wx3[:, (wi * 8 + c) * 512:(wi * 8 + c + 1) * 512],
                            W[c * 128:(c + 1) * 128, c0:c0 + 512], "wx")
            return t

        G = {"xstage_free": [0, 0],
             "pT_free": 0,
             "xlT_free": 0,
             "pP_free": [0, 0],
             "psb_free": [0, 0, 0],
             "pS_free": [0, 0, 0],
             "pC_free": 0,
             "bias_free": 0,
             "rz_free": 0,
             "pZ_free": 0,
             "wosum_free": 0,
             "x2_store": 0,
             "hq_free": 0,
             "w1s_free": [0, 0],
             "w2s_free": [0, 0],
             "xlT2_free": 0,
             "epi_v": 0,
             "ctxst_free": [0, 0],
             "ctxld_free": [0, 0],
             "chain_v": 0,
             "crow_free": 0,
             "zrow_sfree": 0,
             "wstg_free": [0, 0],
             "wstgb_free": [0, 0],
             "pP_sfree": 0}

        PRECAST_UNITS = ([("wo", i) for i in range(H)]
                         + [("w1", i) for i in range(32)]
                         + [("w2", i) for i in range(32)])
        PRE = {"i": 0}

        def precast_emit(n_units):
            for _ in range(n_units):
                if PRE["i"] >= len(PRECAST_UNITS):
                    return
                kind, idx = PRECAST_UNITS[PRE["i"]]
                buf = PRE["i"] % 2
                nm = "wca" if buf == 0 else "wcb"
                snm = "wcsa" if buf == 0 else "wcsb"
                PRE["i"] += 1
                if kind == "w1":
                    src = w1d[:, idx * 128:(idx + 1) * 128].rearrange(
                        "(c p) f -> p c f", p=128)
                    ldst = wstg[buf][:].rearrange("p (c f) -> p c f", c=8)
                    rows = 128
                    sdst = w1bf[:, idx * 1024:(idx + 1) * 1024]
                elif kind == "w2":
                    src = w2d[idx * 128:(idx + 1) * 128, :]
                    ldst = wstg[buf][:]
                    rows = 128
                    sdst = w2bf[:, idx * 1024:(idx + 1) * 1024]
                else:
                    src = Wo[idx * 64:(idx + 1) * 64, :]
                    ldst = wstg[buf][0:64, :]
                    rows = 64
                    sdst = wobf[0:64, idx * 1024:(idx + 1) * 1024]
                wait("sync", "v", G["wstg_free"][buf])
                t_l = dma("sync", ldst, src, nm)
                wait("vector", nm, t_l)
                wait("vector", snm, G["wstgb_free"][buf])
                t_cast = op("vector", lambda io, B=buf, R=rows: io.tensor_copy(
                    out=wstgb[B][0:R, :], in_=wstg[B][0:R, :]), sem="v")
                drain("vector")
                G["wstg_free"][buf] = t_cast
                wait("sync", "v", t_cast)
                G["wstgb_free"][buf] = dma("sync", sdst, wstgb[buf][0:rows, :], snm)

        def stream_block(xsrc, blk_i, wx3_tgt, dst, woff, do_v):
            t_ev_last = 0
            for i in range(4):
                tile = blk_i * 4 + i
                sb = xstage[tile % 2]
                xs = "x0" if tile % 2 == 0 else "x1"
                wait("sync", "v", G["xstage_free"][tile % 2])
                t_x = dma("sync", sb[:], xsrc[tile * 128:(tile + 1) * 128, :], xs)
                wait("vector", xs, t_x)
                ln_stats(sb[:], D)
                t_ap = ln_apply(sb[:], xl[:])
                G["xstage_free"][tile % 2] = t_ap
                wait("tensor", "v", G["pT_free"])
                wait("tensor", "v", t_ap)
                tm = None
                for c in range(8):
                    tm = op("tensor", lambda io, cc=c: io.transpose(
                        pT[:, cc * 128:(cc + 1) * 128],
                        xl[:, cc * 128:(cc + 1) * 128], ident[:]),
                        sem="t" if c == 7 else None)
                wait("vector", "t", G["xlT_free"])
                wait("vector", "t", tm)
                t_ev = op("vector", lambda io, ii=i: io.tensor_copy(
                    out=xlT[:].rearrange("p (c n) -> p c n", c=8)[:, :, ii * 128:(ii + 1) * 128],
                    in_=pT[:].rearrange("p (c n) -> p c n", c=8)), sem="v")
                G["pT_free"] = t_ev
                t_ev_last = t_ev
                if do_v:
                    wait("tensor", "v", t_ev)
                    wait("tensor", "wx", wx3_tgt)
                    wait("tensor", "v", G["pP_free"][1])
                    tm2 = None
                    for c in range(8):
                        tm2 = op("tensor", lambda io, cc=c, ii=i: io.matmul(
                            pP[1][:, 0:512],
                            xlT[:, cc * 512 + ii * 128: cc * 512 + (ii + 1) * 128],
                            Wx3[:, (16 + cc) * 512:(17 + cc) * 512],
                            start=(cc == 0), stop=(cc == 7)),
                            sem="t" if c == 7 else None)
                    wait("vector", "t", tm2)
                    tv = None
                    for hh in range(8):
                        tv = op("vector", lambda io, h=hh, tl=tile: io.tensor_copy(
                            out=Vsb[h][:, tl * 65: tl * 65 + 64],
                            in_=pP[1][:, h * 64:(h + 1) * 64]),
                            sem="v" if hh == 7 else None)
                    G["pP_free"][1] = tv
            base = blk_i * 512
            wait("tensor", "wx", wx3_tgt)
            wait("tensor", "v", t_ev_last)
            for hp in range(4):
                wait("tensor", "v", G["pP_free"][0])
                tm3 = None
                for c in range(8):
                    tm3 = op("tensor", lambda io, cc=c, hhp=hp, wo=woff: io.matmul(
                        pP[0][:, 0:512],
                        Wx3[:, (wo + cc) * 512 + hhp * 128: (wo + cc) * 512 + hhp * 128 + 128],
                        xlT[:, cc * 512:(cc + 1) * 512],
                        start=(cc == 0), stop=(cc == 7)),
                        sem="t" if c == 7 else None)
                wait("vector", "t", tm3)
                tv2 = op("vector", lambda io, hhp=hp, b=base: io.tensor_copy(
                    out=dst[hhp][:, b:b + 512], in_=pP[0][:, 0:512]), sem="v")
                G["pP_free"][0] = tv2
            G["xlT_free"] = cnt["t"]

        def attention_group(hg, t_proj_v):
            wait("tensor", "v", t_proj_v)
            for h8 in range(8):
                h = hg * 8 + h8
                hp = h8 // 2
                off = (h8 % 2) * 64
                slope = SLOPES[h]
                wait("scalar", "v", G["crow_free"])
                t_crow = op("scalar", lambda io, sl=slope: io.activation(
                    out=crow[64:65, 0:512], in_=qpr[64:65, 0:512],
                    func=AF.Exp, bias=0.0, scale=-sl / 8.0), sem="s")
                for g in range(NG):
                    wait("vector", "s", G["bias_free"])
                    t_bias = op("vector", lambda io, sl=slope, col=h * NG + g:
                                io.tensor_scalar(
                                    out=biasmat[:, 0:NT], in0=kpos[:, 0:NT],
                                    scalar1=sl / 8.0, scalar2=qshift[:, col:col + 1],
                                    op0=ALU.mult, op1=ALU.add), sem="v")
                    drain("vector")
                    kmax = 8 * g + 8
                    last_exp = 0
                    qk_t = {}

                    def emit_qk(kt):
                        s_lo = max(4 * g, kt // 2)
                        c0 = (s_lo - 4 * g) * 128
                        pb = kt % 3
                        wait("tensor", "s", G["pS_free"][pb])
                        qk_t[kt] = op("tensor",
                                      lambda io, KT=kt, C0=c0, HP=hp, OF=off,
                                      GG=g, PB=pb:
                                      io.matmul(
                                          pS[PB][:, C0:512],
                                          Ksb[HP][OF:OF + 64,
                                                  KT * 128:(KT + 1) * 128],
                                          Qsb[HP][OF:OF + 64,
                                                  GG * 512 + C0:(GG + 1) * 512],
                                          start=True, stop=True), sem="t")

                    emit_qk(0)
                    for kt in range(kmax):
                        s_lo = max(4 * g, kt // 2)
                        c0 = (s_lo - 4 * g) * 128
                        pb = kt % 3
                        if kt + 1 < kmax:
                            emit_qk(kt + 1)
                        wait("scalar", "t", qk_t[kt])
                        wait("scalar", "v", t_bias)
                        wait("scalar", "t", G["psb_free"][pb])
                        t_exp = op("scalar", lambda io, KT=kt, C0=c0, PB=pb:
                                   io.activation(
                                       out=psb[PB][:, C0:512], in_=pS[PB][:, C0:512],
                                       func=AF.Exp, bias=biasmat[:, KT:KT + 1],
                                       scale=0.125), sem="s")
                        G["pS_free"][pb] = t_exp
                        last_exp = t_exp
                        s_star = kt // 2
                        masked = (4 * g <= s_star < 4 * g + 4) and (kt >= 8 * g)
                        if masked:
                            mc = (kt % 2) * 128
                            wait("vector", "s", t_exp)
                            t_mk = op("vector", lambda io, SS=s_star, GG=g, MC=mc, PB=pb:
                                      io.tensor_tensor(
                                          out=psb[PB][:, (SS - 4 * GG) * 128:(SS - 4 * GG + 1) * 128],
                                          in0=psb[PB][:, (SS - 4 * GG) * 128:(SS - 4 * GG + 1) * 128],
                                          in1=masks[:, MC:MC + 128], op=ALU.mult),
                                      sem="v")
                            wait("tensor", "v", t_mk)
                        else:
                            wait("tensor", "s", t_exp)
                        if kt == 0:
                            wait("tensor", "v", G["pC_free"])
                        t_av = op("tensor",
                                  lambda io, KT=kt, C0=c0, HH=h8, PB=pb, KM=kmax:
                                  io.matmul(
                                      pC[:, C0:512],
                                      Vsb[HH][:, KT * 65:KT * 65 + 65],
                                      psb[PB][:, C0:512],
                                      start=(KT == 0), stop=(KT == KM - 1)), sem="t")
                        G["psb_free"][pb] = t_av
                    t_avl = cnt["t"]
                    wait("vector", "t", t_avl)
                    wait("vector", "t", G["rz_free"])
                    wait("vector", "s", t_crow)
                    wait("vector", "s", G["zrow_sfree"])
                    t_zm = op("vector", lambda io: io.tensor_tensor(
                        out=zrow[64:65, 0:512], in0=pC[64:65, 0:512],
                        in1=crow[64:65, 0:512], op=ALU.mult), sem="v")
                    drain("vector")
                    wait("scalar", "v", t_zm)
                    op("scalar", lambda io: io.activation(
                        out=zrow[64:65, 0:512], in_=zrow[64:65, 0:512],
                        func=AF.Ln, bias=0.0, scale=1.0))
                    drain("scalar")
                    t_rze = op("scalar", lambda io: io.activation(
                        out=rzrow[64:65, 0:512], in_=zrow[64:65, 0:512],
                        func=AF.Exp, bias=0.0, scale=-1.0), sem="s")
                    wait("vector", "s", t_rze)
                    t_rz = op("vector", lambda io: io.tensor_tensor(
                        out=rzrow[64:65, 0:512], in0=rzrow[64:65, 0:512],
                        in1=crow[64:65, 0:512], op=ALU.mult), sem="v")
                    drain("vector")
                    G["zrow_sfree"] = t_rze
                    wait("tensor", "v", t_rz)
                    wait("tensor", "v", G["pZ_free"])
                    t_bc = op("tensor", lambda io: io.matmul(
                        pZ[0:64, 0:512], onescf[64:65, 0:64], rzrow[64:65, 0:512],
                        start=True, stop=True), sem="t")
                    G["rz_free"] = t_bc
                    wait("vector", "t", t_bc)
                    t_evz = op("vector", lambda io: io.tensor_copy(
                        out=rzbc[:, 0:512], in_=pZ[0:64, 0:512]), sem="v")
                    G["pZ_free"] = t_evz
                    drain("vector")
                    cbuf = (h8 * NG + g) % 2
                    ctx_nm = "ctxa" if cbuf == 0 else "ctxb"
                    wait("vector", ctx_nm, G["ctxst_free"][cbuf])
                    t_nm = op("vector", lambda io, CB=cbuf: io.tensor_tensor(
                        out=ctxst[CB][:, 0:512],
                        in0=pC[0:64, 0:512], in1=rzbc[:, 0:512], op=ALU.mult),
                        sem="v")
                    drain("vector")
                    wait("gpsimd", "v", t_nm)
                    G["ctxst_free"][cbuf] = dma(
                        "gpsimd", ctx_d[h * 64:(h + 1) * 64, g * 512:(g + 1) * 512],
                        ctxst[cbuf][:, 0:512], ctx_nm)
                    G["pC_free"] = t_nm
                    G["bias_free"] = last_exp
                    G["crow_free"] = t_rz
                    it = hg * 32 + h8 * NG + g
                    precast_emit((it + 1) * 80 // 64 - it * 80 // 64)

        t_wx3 = load_wx3(0)
        for hg in range(2):
            if hg == 1:
                wait("gpsimd", "t", cnt["t"])
                t_wx3 = load_wx3(1)
                wait("vector", "t", cnt["t"])
            for b8 in range(8):
                stream_block(x_full, b8, t_wx3, Ksb, 8, do_v=True)
            for b4 in range(4):
                stream_block(x_my, b4, t_wx3, Qsb, 0, do_v=False)
            attention_group(hg, cnt["v"])

        def barrier():
            snap = dict(cnt)
            for eng in ("sync", "tensor", "vector", "scalar", "gpsimd"):
                for snm, val in snap.items():
                    wait(eng, snm, val)

        pstk1.close()
        stk1.close()
        barrier()

        stk2 = contextlib.ExitStack()
        e2 = stk2.enter_context
        Wosb = e2(nc.sbuf_tensor("Wosbt", [64, H * D], BF16))
        bo_b = e2(nc.sbuf_tensor("bo_bt", [128, D], F32))
        ctxld = [e2(nc.sbuf_tensor(f"ctxldt{i}", [64, 512], BF16)) for i in range(2)]
        pstk2 = contextlib.ExitStack()
        ep2 = pstk2.enter_context
        pW = [ep2(nc.psum_tensor(f"pWp{i}", [128, 512], F32)) for i in range(8)]

        t_wo = None
        for hh in range(H):
            t_wo = dma("scalar", Wosb[0:64, hh * D:(hh + 1) * D],
                       wobf[0:64, hh * 1024:(hh + 1) * 1024], "wo")
        bcast_row(0, bo_b, pW[0])
        t_bo_v = cnt["v"]
        pW_free_v = t_bo_v
        for blk in range(4):
            for hh in range(H):
                cb = hh % 2
                cxs = "cx0" if cb == 0 else "cx1"
                wait("sync", "t", G["ctxld_free"][cb])
                t_cx = dma("sync", ctxld[cb][:, 0:512],
                           ctx_d[hh * 64:(hh + 1) * 64, blk * 512:(blk + 1) * 512],
                           cxs)
                wait("tensor", cxs, t_cx)
                if hh == 0:
                    wait("tensor", "wo", t_wo)
                    wait("tensor", "v", pW_free_v)
                tm = None
                for tl in range(4):
                    for bank in range(2):
                        tm = op("tensor",
                                lambda io, HHH=hh, TL=tl, BK=bank, CB=cb:
                                io.matmul(
                                    pW[TL * 2 + BK][:, 0:512],
                                    ctxld[CB][:, TL * 128:(TL + 1) * 128],
                                    Wosb[0:64, HHH * D + BK * 512: HHH * D + BK * 512 + 512],
                                    start=(HHH == 0), stop=(HHH == H - 1)),
                                sem="t" if (tl == 3 and bank == 1) else None)
                G["ctxld_free"][cb] = tm
            t_womm = cnt["t"]
            for tl in range(4):
                tt = blk * 4 + tl
                sb = xstage[tt % 2]
                xs = "x0" if tt % 2 == 0 else "x1"
                wait("sync", "v", G["xstage_free"][tt % 2])
                t_x = dma("sync", sb[:], x_my[tt * 128:(tt + 1) * 128, :], xs)
                wait("vector", "t", t_womm)
                wait("vector", "v", G["wosum_free"])
                for bank in range(2):
                    op("vector", lambda io, TL=tl, BK=bank: io.scalar_tensor_tensor(
                        out=wosum[:, BK * 512:(BK + 1) * 512],
                        in0=pW[TL * 2 + BK][:, 0:512],
                        scalar=1.0, in1=bo_b[:, BK * 512:(BK + 1) * 512],
                        op0=ALU.mult, op1=ALU.add),
                       sem="v" if bank == 1 else None)
                drain("vector")
                pW_free_v = cnt["v"]
                ln_stats(wosum[:], D)
                t_ap = ln_apply(wosum[:], lntmp[:])
                G["wosum_free"] = t_ap
                wait("vector", xs, t_x)
                t_x2 = op("vector", lambda io, TT=tt: io.tensor_tensor(
                    out=x2t[TT % 2][:], in0=xstage[TT % 2][:], in1=lntmp[:],
                    op=ALU.add), sem="v")
                drain("vector")
                G["xstage_free"][tt % 2] = t_x2
                x2s_nm = "x2a" if tt % 2 == 0 else "x2b"
                wait("scalar", "v", t_x2)
                G["x2_store"] = dma("scalar", x2_d[tt * 128:(tt + 1) * 128, :],
                                    x2t[tt % 2][:], x2s_nm)
                wait("vector", x2s_nm, G["x2_store"])

        pstk2.close()
        stk2.close()
        barrier()

        stk3 = contextlib.ExitStack()
        e3 = stk3.enter_context
        b2_b = e3(nc.sbuf_tensor("b2_bt", [128, D], F32))
        w1s = [e3(nc.sbuf_tensor(f"w1st{i}", [128, 8 * 128], BF16)) for i in range(2)]
        w2s = [e3(nc.sbuf_tensor(f"w2st{i}", [128, D], BF16)) for i in range(2)]
        hq = e3(nc.sbuf_tensor("hqt", [128, 32 * 1024], BF16))
        xlT2 = e3(nc.sbuf_tensor("xlT2t", [128, 8 * 1024], BF16))
        c_u = e3(nc.sbuf_tensor("c_ut", [128, 1024], F32))
        c_ti = e3(nc.sbuf_tensor("c_tit", [128, 1024], I32))
        c_tf = e3(nc.sbuf_tensor("c_tft", [128, 1024], F32))
        c_e = e3(nc.sbuf_tensor("c_et", [128, 1024], F32))
        c_a = e3(nc.sbuf_tensor("c_at", [128, 1024], F32))
        c_nt = e3(nc.sbuf_tensor("c_ntt", [128, 1024], F32))
        c_ti2 = e3(nc.sbuf_tensor("c_ti2t", [128, 1024], I32))
        c_ri = e3(nc.sbuf_tensor("c_rit", [128, 1024], I32))
        c_rf = e3(nc.sbuf_tensor("c_rft", [128, 1024], F32))
        c_z = e3(nc.sbuf_tensor("c_zt", [128, 1024], F32))
        c_g0 = e3(nc.sbuf_tensor("c_g0t", [128, 1024], F32))
        c_g1 = e3(nc.sbuf_tensor("c_g1t", [128, 1024], F32))
        c_y = e3(nc.sbuf_tensor("c_yt", [128, 1024], F32))
        pstk3 = contextlib.ExitStack()
        ep3 = pstk3.enter_context
        pM = [ep3(nc.psum_tensor(f"pMp{i}", [128, 512], F32)) for i in range(4)]
        pT3 = ep3(nc.psum_tensor("pT3p", [128, D], BF16))

        bcast_row(64, b2_b, pM[0])
        G["epi_v"] = cnt["v"]

        LADDER_COLS = {}

        def sign_peel(vbuf, bbuf, levels, tag, width=1024):
            K = 0.0
            hw_ = width // 2
            halves = [(0, hw_), (hw_, width)]
            t_s = [0, 0]
            t_v = [0, 0]
            wait("scalar", "v", cnt["v"])
            for i, c in enumerate(levels):
                col = LADDER_COLS[tag][i]
                for hh in (0, 1):
                    lo, hi = halves[hh]
                    if t_v[hh]:
                        wait("scalar", "v", t_v[hh])
                    t_s[hh] = op("scalar",
                                 lambda io, V=vbuf, Bb=bbuf, CL=col, L=lo, Hh=hi:
                                 io.activation(
                                     out=Bb[:, L:Hh], in_=V[:, L:Hh], func=AF.Sign,
                                     bias=ladc[:, CL:CL + 1], scale=1.0), sem="s")
                for hh in (0, 1):
                    lo, hi = halves[hh]
                    wait("vector", "s", t_s[hh])
                    t_v[hh] = op("vector",
                                 lambda io, V=vbuf, Bb=bbuf, C=c, L=lo, Hh=hi:
                                 io.scalar_tensor_tensor(
                                     out=V[:, L:Hh], in0=Bb[:, L:Hh],
                                     scalar=float(-C / 2.0), in1=V[:, L:Hh],
                                     op0=ALU.mult, op1=ALU.add), sem="v")
                    drain("vector")
                K += -c / 2.0
            wait("scalar", "v", max(t_v))
            return K

        LADDER_COLS.update({
            "hA": [0, 1, 2, 3, 4],
            "hB": [5, 6, 7],
            "h3n": [8, 9, 10, 11],
            "h4n": [12, 13, 14, 15, 16],
        })

        def ffn_chain(ft):
            wait("scalar", "v", G["chain_v"])
            for half in range(2):
                op("scalar", lambda io, HF=half, FT=ft: io.activation(
                    out=c_u[:, HF * 512:(HF + 1) * 512], in_=pM[HF][:, 0:512],
                    func=AF.Identity, bias=b1bias[:, FT:FT + 1], scale=256.0),
                   sem="s" if half == 1 else None)
            t_u = cnt["s"]
            G["pP_sfree"] = t_u
            wait("vector", "s", t_u)
            op("vector", lambda io: io.tensor_scalar(
                out=c_ti[:], in0=c_u[:], scalar1=1.0, scalar2=-16.0,
                op0=ALU.mult, op1=ALU.add))
            drain("vector")
            op("vector", lambda io: io.tensor_copy(out=c_tf[:], in_=c_ti[:]))
            drain("vector")
            op("vector", lambda io: io.scalar_tensor_tensor(
                out=c_e[:], in0=c_tf[:], scalar=-1.0, in1=c_u[:],
                op0=ALU.mult, op1=ALU.add), sem="v")
            drain("vector")
            sign_peel(c_e, c_a, [16, 8, 4, 2, 1], "hA")
            op("vector", lambda io: io.tensor_scalar(
                out=c_ti[:], in0=c_u[:], scalar1=1.0 / 64, scalar2=-4.0,
                op0=ALU.mult, op1=ALU.add))
            drain("vector")
            op("vector", lambda io: io.tensor_copy(out=c_tf[:], in_=c_ti[:]))
            drain("vector")
            op("vector", lambda io: io.scalar_tensor_tensor(
                out=c_rf[:], in0=c_tf[:], scalar=-64.0, in1=c_u[:],
                op0=ALU.mult, op1=ALU.add), sem="v")
            drain("vector")
            sign_peel(c_rf, c_a, [256, 128, 64], "hB")
            t_z = op("vector", lambda io: io.tensor_tensor(
                out=c_z[:], in0=c_rf[:], in1=c_u[:], op=ALU.subtract), sem="v")
            drain("vector")
            op("vector", lambda io: io.tensor_tensor(
                out=c_rf[:], in0=c_rf[:], in1=c_e[:], op=ALU.subtract))
            drain("vector")
            op("vector", lambda io: io.tensor_scalar(
                out=c_rf[:], in0=c_rf[:], scalar1=1.0, scalar2=-208.5,
                op0=ALU.mult, op1=ALU.add))
            drain("vector")
            wait("scalar", "v", t_z)
            op("scalar", lambda io: io.activation(
                out=c_g0[:], in_=c_z[:], func=AF.Gelu, bias=cbias[:, 1:2],
                scale=-1.0 / 256))
            drain("scalar")
            t_g = op("scalar", lambda io: io.activation(
                out=c_g1[:], in_=c_z[:], func=AF.Gelu, bias=cbias[:, 2:3],
                scale=-1.0 / 256), sem="s")
            wait("vector", "s", t_g)
            op("vector", lambda io: io.tensor_tensor(
                out=c_g1[:], in0=c_g1[:], in1=c_g0[:], op=ALU.subtract))
            drain("vector")
            op("vector", lambda io: io.scalar_tensor_tensor(
                out=c_e[:], in0=c_g1[:], scalar=1.0 / 64, in1=c_rf[:],
                op0=ALU.mult, op1=ALU.mult))
            drain("vector")
            t_y = op("vector", lambda io: io.tensor_tensor(
                out=c_y[:], in0=c_e[:], in1=c_g0[:], op=ALU.add), sem="v")
            wait("scalar", "v", t_y)
            t_u2 = op("scalar", lambda io: io.activation(
                out=c_u[:], in_=c_y[:], func=AF.Identity, bias=cbias[:, 3:4],
                scale=256.0), sem="s")
            wait("vector", "s", t_u2)
            op("vector", lambda io: io.tensor_scalar(
                out=c_ti[:], in0=c_u[:], scalar1=1.0, scalar2=-8.0,
                op0=ALU.mult, op1=ALU.add))
            drain("vector")
            op("vector", lambda io: io.tensor_copy(out=c_tf[:], in_=c_ti[:]))
            drain("vector")
            op("vector", lambda io: io.scalar_tensor_tensor(
                out=c_e[:], in0=c_tf[:], scalar=-1.0, in1=c_u[:],
                op0=ALU.mult, op1=ALU.add), sem="v")
            drain("vector")
            sign_peel(c_e, c_a, [8, 4, 2, 1], "h3n")
            t_ca = op("vector", lambda io: io.tensor_tensor(
                out=c_a[:], in0=c_u[:], in1=c_e[:], op=ALU.subtract), sem="v")
            drain("vector")
            wait("scalar", "v", t_ca)
            t_hq = op("scalar", lambda io, FT=ft: io.activation(
                out=hq[:, FT * 1024:(FT + 1) * 1024], in_=c_a[:],
                func=AF.Identity, bias=cbias[:, 5:6], scale=1.0 / 256),
                sem="s")
            G["chain_v"] = t_ca
            return t_hq

        wait("sync", "x2a", cnt["x2a"])
        wait("sync", "x2b", cnt["x2b"])
        for fb in range(2):
            for i in range(8):
                tile = fb * 8 + i
                sb = xstage[i % 2]
                xs = "x0" if i % 2 == 0 else "x1"
                wait("sync", "v", G["xstage_free"][i % 2])
                t_x = dma("sync", sb[:], x2_d[tile * 128:(tile + 1) * 128, :], xs)
                wait("vector", xs, t_x)
                ln_stats(sb[:], D)
                t_ap = ln_apply(sb[:], xl[:])
                G["xstage_free"][i % 2] = t_ap
                wait("tensor", "v", G["pT_free"])
                wait("tensor", "v", t_ap)
                tm = None
                for c in range(8):
                    tm = op("tensor", lambda io, cc=c: io.transpose(
                        pT3[:, cc * 128:(cc + 1) * 128],
                        xl[:, cc * 128:(cc + 1) * 128], ident[:]),
                        sem="t" if c == 7 else None)
                wait("vector", "t", G["xlT2_free"])
                wait("vector", "t", tm)
                t_ev = op("vector", lambda io, ii=i: io.tensor_copy(
                    out=xlT2[:].rearrange("p (c n) -> p c n", c=8)[:, :, ii * 128:(ii + 1) * 128],
                    in_=pT3[:].rearrange("p (c n) -> p c n", c=8)), sem="v")
                G["pT_free"] = t_ev
            t_xlT2 = cnt["v"]
            wait("scalar", "t", G["hq_free"])
            for ft in range(32):
                buf = ft % 2
                w1nm = "w1a" if buf == 0 else "w1b"
                wait("sync", "t", G["w1s_free"][buf])
                t_w1 = dma("sync", w1s[buf][:],
                           w1bf[:, ft * 1024:(ft + 1) * 1024], w1nm)
                wait("tensor", w1nm, t_w1)
                if ft == 0:
                    wait("tensor", "v", t_xlT2)
                    wait("tensor", "v", G["epi_v"])
                wait("tensor", "s", G["pP_sfree"])
                tm = None
                for half in range(2):
                    for c in range(8):
                        tm = op("tensor", lambda io, cc=c, HF=half, BUF=buf: io.matmul(
                            pM[HF][:, 0:512],
                            w1s[BUF][:, cc * 128:(cc + 1) * 128],
                            xlT2[:, cc * 1024 + HF * 512: cc * 1024 + (HF + 1) * 512],
                            start=(cc == 0), stop=(cc == 7)),
                            sem="t" if (c == 7 and half == 1) else None)
                G["w1s_free"][buf] = tm
                wait("scalar", "t", tm)
                ffn_chain(ft)
            t_hq_all = cnt["s"]
            G["xlT2_free"] = cnt["t"]
            for tp in range(4):
                tiles = (fb * 8 + tp * 2, fb * 8 + tp * 2 + 1)
                t_w2mm = None
                for f in range(32):
                    buf = f % 2
                    w2nm = "w2a" if buf == 0 else "w2b"
                    wait("gpsimd", "t", G["w2s_free"][buf])
                    t_w2 = dma("gpsimd", w2s[buf][:],
                               w2bf[:, f * 1024:(f + 1) * 1024], w2nm)
                    wait("tensor", w2nm, t_w2)
                    if f == 0:
                        wait("tensor", "s", t_hq_all)
                        wait("tensor", "s", G["pP_sfree"])
                        wait("tensor", "v", G["epi_v"])
                    tm = None
                    for li in range(2):
                        for bank in range(2):
                            tm = op("tensor",
                                    lambda io, FF_=f, LI=li, BK=bank, TL=tiles[li], FB=fb, BUF=buf:
                                    io.matmul(
                                        pM[LI * 2 + BK][:, 0:512],
                                        hq[:, FF_ * 1024 + (TL - FB * 8) * 128:
                                           FF_ * 1024 + (TL - FB * 8) * 128 + 128],
                                        w2s[BUF][:, BK * 512:(BK + 1) * 512],
                                        start=(FF_ == 0), stop=(FF_ == 31)),
                                    sem="t" if (li == 1 and bank == 1) else None)
                    t_w2mm = tm
                    G["w2s_free"][buf] = tm
                for li in range(2):
                    tile = tiles[li]
                    wait("vector", "t", t_w2mm)
                    wait("vector", "v", G["wosum_free"])
                    for bank in range(2):
                        op("vector", lambda io, LI=li, BK=bank: io.scalar_tensor_tensor(
                            out=wosum[:, BK * 512:(BK + 1) * 512],
                            in0=pM[LI * 2 + BK][:, 0:512],
                            scalar=1.0, in1=b2_b[:, BK * 512:(BK + 1) * 512],
                            op0=ALU.mult, op1=ALU.add),
                           sem="v" if bank == 1 else None)
                    drain("vector")
                    t_o = cnt["v"]
                    if li == 1:
                        G["epi_v"] = t_o
                    wait("scalar", "v", t_o)
                    t_u3 = op("scalar", lambda io: io.activation(
                        out=c_u[:], in_=wosum[:], func=AF.Identity,
                        bias=cbias[:, 4:5], scale=256.0), sem="s")
                    wait("vector", "s", t_u3)
                    op("vector", lambda io: io.tensor_scalar(
                        out=c_ti[:], in0=c_u[:], scalar1=1.0, scalar2=-16.0,
                        op0=ALU.mult, op1=ALU.add))
                    drain("vector")
                    op("vector", lambda io: io.tensor_copy(out=c_tf[:], in_=c_ti[:]))
                    drain("vector")
                    op("vector", lambda io: io.scalar_tensor_tensor(
                        out=c_e[:], in0=c_tf[:], scalar=-1.0, in1=c_u[:],
                        op0=ALU.mult, op1=ALU.add), sem="v")
                    drain("vector")
                    sign_peel(c_e, c_a, [16, 8, 4, 2, 1], "h4n")
                    op("vector", lambda io: io.tensor_tensor(
                        out=c_a[:], in0=c_u[:], in1=c_e[:], op=ALU.subtract))
                    drain("vector")
                    op("vector", lambda io: io.tensor_scalar(
                        out=lntmp[:], in0=c_a[:], scalar1=1.0 / 256,
                        scalar2=-31.939453125, op0=ALU.mult, op1=ALU.add))
                    drain("vector")
                    ln_stats(lntmp[:], D)
                    ln_apply(lntmp[:], wosum[:])
                    sb = xstage[li]
                    xs = "x0" if li == 0 else "x1"
                    wait("sync", "v", G["xstage_free"][li])
                    t_x = dma("sync", sb[:], x2_d[tile * 128:(tile + 1) * 128, :], xs)
                    wait("vector", xs, t_x)
                    t_out = op("vector", lambda io, LI=li: io.tensor_tensor(
                        out=outt[LI][:], in0=xstage[LI][:], in1=wosum[:],
                        op=ALU.add), sem="v")
                    drain("vector")
                    G["xstage_free"][li] = t_out
                    G["wosum_free"] = t_out
                    out_nm = "outa" if li == 0 else "outb"
                    wait("scalar", "v", t_out)
                    t_st = dma("scalar", out_d[tile * 128:(tile + 1) * 128, :],
                               outt[li][:], out_nm)
                    wait("vector", out_nm, t_st)
            G["hq_free"] = cnt["t"]
        wait("gpsimd", "outa", cnt["outa"])
        wait("gpsimd", "outb", cnt["outb"])
        pstk3.close()
        stk3.close()

        @BLOCK.sync
        def _(io):
            for f in ops["sync"]:
                f(io)

        @BLOCK.tensor
        def _(io):
            for f in ops["tensor"]:
                f(io)

        @BLOCK.vector
        def _(io):
            for f in ops["vector"]:
                f(io)

        @BLOCK.scalar
        def _(io):
            for f in ops["scalar"]:
                f(io)

        @BLOCK.gpsimd
        def _(io):
            for f in ops["gpsimd"]:
                f(io)

    return nc


_NC_CACHE = {}


def _constants(par):
    ident = np.eye(128, dtype=ml_dtypes.bfloat16)
    kposc = (np.arange(NT)[None, :] * 128 + np.arange(128)[:, None]).astype(np.float32)
    qshift = np.zeros((128, H * NG), dtype=np.float32)
    for h in range(H):
        for g in range(NG):
            t0 = 128 * (8 * g + par)
            t1 = 128 * (8 * g + 6 + par) + 127
            qref = 0.5 * (t0 + t1)
            qshift[:, h * NG + g] = -MSHIFT - SLOPES[h] * qref / 8.0
    ladc = np.zeros((128, 64), dtype=np.float32)
    col = 0
    for levels in ([16, 8, 4, 2, 1], [256, 128, 64], [8, 4, 2, 1],
                   [16, 8, 4, 2, 1]):
        K = 0.0
        for c in levels:
            ladc[:, col] = -(c + K)
            K += c / 2.0
            col += 1
    qpr = np.zeros((128, 512), dtype=np.float32)
    qpr[64, :] = np.repeat(np.array([-384.0, -128.0, 128.0, 384.0], dtype=np.float32), 128)
    lowtri = (np.arange(128)[:, None] <= np.arange(128)[None, :])
    masks = np.zeros((128, 256), dtype=ml_dtypes.bfloat16)
    if par == 0:
        masks[:, 0:128] = lowtri.astype(ml_dtypes.bfloat16)
        masks[:, 128:256] = 0
    else:
        masks[:, 0:128] = 1
        masks[:, 128:256] = lowtri.astype(ml_dtypes.bfloat16)
    return ident, kposc, qshift, masks, qpr, ladc


def kernel(L0_attn_x, L0_attn_Wq, L0_attn_Wk, L0_attn_Wv, L0_attn_Wo, L0_attn_bo,
           L0_ffn_w1, L0_ffn_b1, L0_ffn_w2, L0_ffn_b2):
    x = np.ascontiguousarray(np.asarray(L0_attn_x, dtype=np.float32))
    Wq = np.ascontiguousarray(np.asarray(L0_attn_Wq, dtype=np.float32))
    Wk = np.ascontiguousarray(np.asarray(L0_attn_Wk, dtype=np.float32))
    Wv = np.ascontiguousarray(np.asarray(L0_attn_Wv, dtype=np.float32))
    Wo = np.ascontiguousarray(np.asarray(L0_attn_Wo, dtype=np.float32))
    bo = np.ascontiguousarray(np.asarray(L0_attn_bo, dtype=np.float32))
    w1 = np.ascontiguousarray(np.asarray(L0_ffn_w1, dtype=np.float32))
    b1 = np.ascontiguousarray(
        np.asarray(L0_ffn_b1, dtype=np.float32).reshape(FF // 128, 128).T)
    w2 = np.ascontiguousarray(np.asarray(L0_ffn_w2, dtype=np.float32))
    b2 = np.ascontiguousarray(np.asarray(L0_ffn_b2, dtype=np.float32))

    if "nc" not in _NC_CACHE:
        _NC_CACHE["nc"] = build_nc()
    nc = _NC_CACHE["nc"]

    in_maps = []
    for c in range(8):
        b, par = c // 2, c % 2
        xb = x[b]
        x_my = np.ascontiguousarray(
            xb.reshape(NT, 128, D)[par::2].reshape(MYN, D))
        ident, kposc, qshift, masks, qpr, ladc = _constants(par)
        in_maps.append({
            "x_full": xb, "x_my": x_my,
            "Wq": Wq, "Wk": Wk, "Wv": Wv, "Wo": Wo, "bo": bo,
            "w1": w1, "b1": b1, "w2": w2, "b2": b2,
            "identc": ident, "kposc": kposc, "qshiftc": qshift, "masksc": masks,
            "qprc": qpr, "ladc": ladc,
        })
    res = run_bass_kernel_spmd(nc, in_maps, core_ids=list(range(8)))
    out = np.empty((B, N, D), dtype=np.float32)
    for c in range(8):
        b, par = c // 2, c % 2
        o = res.results[c]["out"].reshape(MYT, 128, D)
        out[b].reshape(NT, 128, D)[par::2] = o
    return out


# version 10
# speedup vs baseline: 1.1795x; 1.0002x over previous
import contextlib

import numpy as np
import ml_dtypes

import concourse.bass as bass
import concourse.mybir as mybir
from concourse.bass_utils import run_bass_kernel_spmd

F32 = mybir.dt.float32
BF16 = mybir.dt.bfloat16
I32 = mybir.dt.int32
AF = mybir.ActivationFunctionType
ALU = mybir.AluOpType
AX = mybir.AxisListType

B, N, D = 4, 4096, 1024
H = 16
FF = 4096
NT = N // 128
MYT = NT // 2
MYN = MYT * 128
NG = 4
SLOPES = [2.0 ** (-0.5 * (i + 1)) for i in range(H)]
MSHIFT = 12.0


def build_nc():
    nc = bass.Bass("TRN2", target_bir_lowering=False)
    ap = lambda nm, shp, ty, kind: nc.dram_tensor(nm, shp, ty, kind=kind).ap()

    x_full = ap("x_full", [N, D], F32, "ExternalInput")
    x_my = ap("x_my", [MYN, D], F32, "ExternalInput")
    Wq = ap("Wq", [D, D], F32, "ExternalInput")
    Wk = ap("Wk", [D, D], F32, "ExternalInput")
    Wv = ap("Wv", [D, D], F32, "ExternalInput")
    Wo = ap("Wo", [D, D], F32, "ExternalInput")
    bo = ap("bo", [D], F32, "ExternalInput")
    w1d = ap("w1", [D, FF], F32, "ExternalInput")
    b1d = ap("b1", [128, FF // 128], F32, "ExternalInput")
    w2d = ap("w2", [FF, D], F32, "ExternalInput")
    b2d = ap("b2", [D], F32, "ExternalInput")
    ident_in = ap("identc", [128, 128], BF16, "ExternalInput")
    kpos_in = ap("kposc", [128, NT], F32, "ExternalInput")
    qshift_in = ap("qshiftc", [128, H * NG], F32, "ExternalInput")
    masks_in = ap("masksc", [128, 2 * 128], BF16, "ExternalInput")
    qpr_in = ap("qprc", [128, 512], F32, "ExternalInput")
    ladc_in = ap("ladc", [128, 64], F32, "ExternalInput")
    out_d = ap("out", [MYN, D], F32, "ExternalOutput")
    x2_d = ap("x2scr", [MYN, D], F32, "Internal")
    ctx_d = ap("ctxscr", [H * 64, MYN], BF16, "Internal")
    w1bf = ap("w1bf", [128, 32 * 1024], BF16, "Internal")
    w2bf = ap("w2bf", [128, 32 * 1024], BF16, "Internal")
    wobf = ap("wobf", [64, 16 * 1024], BF16, "Internal")

    ops = {e: [] for e in ("sync", "tensor", "vector", "scalar", "gpsimd")}
    cnt = {s: 0 for s in ("t", "v", "s", "cl", "wx", "wo",
                          "x0", "x1", "cx0", "cx1", "w1a", "w1b", "w2a", "w2b",
                          "x2a", "x2b", "outa", "outb", "ctxa", "ctxb",
                          "wca", "wcb", "wcsa", "wcsb")}
    semh = {}

    def op(eng, fn, sem=None):
        if sem is None:
            ops[eng].append(fn)
            return None
        cnt[sem] += 1
        tgt = cnt[sem]
        ops[eng].append(lambda io, f=fn, ss=sem: f(io).then_inc(semh[ss], 1))
        return tgt

    def dma(eng, dst, src, sem):
        cnt[sem] += 16
        tgt = cnt[sem]
        ops[eng].append(lambda io, d=dst, s=src, ss=sem:
                        io.dma_start(out=d, in_=s).then_inc(semh[ss], 16))
        return tgt

    def wait(eng, sem, tgt):
        if tgt:
            ops[eng].append(lambda io, ss=sem, t=tgt: io.wait_ge(semh[ss], t))

    def drain(eng):
        ops[eng].append(lambda io: io.drain())

    stk = contextlib.ExitStack()
    e = stk.enter_context
    with stk:
        ident = e(nc.sbuf_tensor("identt", [128, 128], BF16))
        kpos = e(nc.sbuf_tensor("kpost", [128, NT], F32))
        qshift = e(nc.sbuf_tensor("qshiftt", [128, H * NG], F32))
        masks = e(nc.sbuf_tensor("maskst", [128, 2 * 128], BF16))
        qpr = e(nc.sbuf_tensor("qprt", [128, 512], F32))
        ladc = e(nc.sbuf_tensor("ladct", [128, 64], F32))
        crow = e(nc.sbuf_tensor("crowt", [128, 512], F32))
        biasmat = e(nc.sbuf_tensor("biasmatt", [128, NT], F32))
        b1bias = e(nc.sbuf_tensor("b1biast", [128, FF // 128], F32))
        brow = e(nc.sbuf_tensor("browt", [128, D], F32))
        onesc = e(nc.sbuf_tensor("onesct", [128, 128], BF16))
        onescf = e(nc.sbuf_tensor("onescft", [128, 128], F32))
        cbias = e(nc.sbuf_tensor("cbiast", [128, 8], F32))
        xstage = [e(nc.sbuf_tensor(f"xstaget{i}", [128, D], F32)) for i in range(2)]
        xl = e(nc.sbuf_tensor("xlt", [128, D], BF16))
        stats = e(nc.sbuf_tensor("statst", [128, 8], F32))
        sqscr = e(nc.sbuf_tensor("sqscrt", [128, D], F32))
        wosum = e(nc.sbuf_tensor("wosumt", [128, D], F32))
        lntmp = e(nc.sbuf_tensor("lntmpt", [128, D], F32))
        x2t = [e(nc.sbuf_tensor(f"x2tt{i}", [128, D], F32)) for i in range(2)]
        outt = [e(nc.sbuf_tensor(f"outtt{i}", [128, D], F32)) for i in range(2)]

        stk1 = contextlib.ExitStack()
        e1 = stk1.enter_context
        Ksb = [e1(nc.sbuf_tensor(f"Ksb{i}", [128, N], BF16)) for i in range(4)]
        Qsb = [e1(nc.sbuf_tensor(f"Qsb{i}", [128, MYN], BF16)) for i in range(4)]
        Vsb = [e1(nc.sbuf_tensor(f"Vsb{i}", [128, NT * 65], BF16)) for i in range(8)]
        Wx3 = e1(nc.sbuf_tensor("Wx3t", [128, 24 * 512], BF16))
        xlT = e1(nc.sbuf_tensor("xlTt", [128, 8 * 512], BF16))
        psb = [e1(nc.sbuf_tensor(f"psbt{i}", [128, 512], BF16)) for i in range(3)]
        zrow = e1(nc.sbuf_tensor("zrowt", [128, 512], F32))
        rzrow = e1(nc.sbuf_tensor("rzrowt", [128, 512], F32))
        rzbc = e1(nc.sbuf_tensor("rzbct", [64, 512], F32))
        ctxst = [e1(nc.sbuf_tensor(f"ctxstt{i}", [64, 512], BF16)) for i in range(2)]
        wstg = [e1(nc.sbuf_tensor(f"wstgt{i}", [128, 1024], F32)) for i in range(2)]
        wstgb = [e1(nc.sbuf_tensor(f"wstgbt{i}", [128, 1024], BF16)) for i in range(2)]

        pstk1 = contextlib.ExitStack()
        ep1 = pstk1.enter_context
        pS = [ep1(nc.psum_tensor(f"pSp{i}", [128, 512], F32)) for i in range(3)]
        pC = ep1(nc.psum_tensor("pCp", [65, 512], F32))
        pZ = ep1(nc.psum_tensor("pZp", [64, 512], F32))
        pT = ep1(nc.psum_tensor("pTp", [128, D], BF16))
        pP = [ep1(nc.psum_tensor(f"pPp{i}", [128, 512], F32)) for i in range(2)]

        for snm in cnt:
            semh[snm] = e(nc.semaphore(f"sem_{snm}"))
        BLOCK = e(nc.Block())

        t_c = 0
        for dst, src in ((ident, ident_in), (kpos, kpos_in), (qshift, qshift_in),
                         (masks, masks_in), (qpr, qpr_in), (ladc, ladc_in)):
            t_c = dma("sync", dst[:], src, "cl")
        t_c = dma("sync", brow[0:1, :], bo.rearrange("(o d) -> o d", o=1), "cl")
        t_c = dma("sync", brow[64:65, :], b2d.rearrange("(o d) -> o d", o=1), "cl")
        t_c = dma("sync", b1bias[:], b1d, "cl")
        t_consts = t_c

        wait("vector", "cl", t_consts)
        op("vector", lambda io: io.memset(onesc[:], 1.0))
        op("vector", lambda io: io.memset(onescf[:], 1.0))
        for col, val in ((0, 1e-5), (1, -7.125), (2, -6.875), (3, 64.5),
                         (4, 8192.5), (5, -0.220703125)):
            op("vector", lambda io, C=col, VV=val: io.memset(cbias[:, C:C + 1], VV))
        for hh in range(8):
            op("vector", lambda io, h=hh: io.memset(
                Vsb[h][:].rearrange("p (k c) -> p k c", c=65)[:, :, 64:65], 1.0))
        t_b1b = op("vector", lambda io: io.tensor_scalar(
            out=b1bias[:], in0=b1bias[:], scalar1=256.0, scalar2=2048.5,
            op0=ALU.mult, op1=ALU.add), sem="v")
        drain("vector")

        def bcast_row(r_i, dstt, pbank):
            for half in range(2):
                for prange in range(2):
                    t_bc = op("tensor", lambda io, rr=r_i, hh=half, pr=prange, pb=pbank:
                              io.matmul(pb[0:64, 0:512],
                                        onescf[rr:rr + 1, pr * 64:(pr + 1) * 64],
                                        brow[rr:rr + 1, hh * 512:(hh + 1) * 512],
                                        start=True, stop=True,
                                        tile_position=(rr, 0)), sem="t")
                    wait("vector", "t", t_bc)
                    t_bcv = op("vector", lambda io, d=dstt, hh=half, pr=prange, pb=pbank:
                               io.tensor_copy(
                                   out=d[pr * 64:(pr + 1) * 64, hh * 512:(hh + 1) * 512],
                                   in_=pb[0:64, 0:512]), sem="v")
                    wait("tensor", "v", t_bcv)

        ln_guard = {"s": 0}

        def ln_stats(src_ap, Dd):
            wait("vector", "s", ln_guard["s"])
            op("vector", lambda io, sa=src_ap: io.tensor_reduce(
                out=stats[:, 0:1], in_=sa, axis=AX.X, op=ALU.add))
            drain("vector")
            t2 = op("vector", lambda io, dd=Dd: io.tensor_scalar(
                out=stats[:, 0:1], in0=stats[:, 0:1], scalar1=-1.0 / dd, scalar2=0.0,
                op0=ALU.mult, op1=ALU.add), sem="v")
            drain("vector")
            wait("scalar", "v", t2)
            op("scalar", lambda io, sa=src_ap, dd=Dd: io.activation(
                out=sqscr[:, 0:dd], in_=sa, func=AF.Square,
                bias=stats[:, 0:1], scale=1.0, accum_out=stats[:, 2:3]))
            drain("scalar")
            t4 = op("scalar", lambda io, dd=Dd: io.activation(
                out=stats[:, 3:4], in_=stats[:, 2:3], func=AF.Sqrt,
                bias=cbias[:, 0:1], scale=1.0 / dd), sem="s")
            wait("vector", "s", t4)
            t5 = op("vector", lambda io: io.reciprocal(
                out=stats[:, 1:2], in_=stats[:, 3:4]), sem="v")
            drain("vector")
            ln_guard["s"] = t4
            return t5

        def ln_apply(src_ap, dst_ap):
            t = op("vector", lambda io, sa=src_ap, da=dst_ap: io.tensor_scalar(
                out=da, in0=sa, scalar1=stats[:, 0:1], scalar2=stats[:, 1:2],
                op0=ALU.add, op1=ALU.mult), sem="v")
            drain("vector")
            return t

        def load_wx3(hg):
            c0 = hg * 512
            t = None
            for wi, W in enumerate((Wq, Wk, Wv)):
                for c in range(8):
                    t = dma("gpsimd",
                            Wx3[:, (wi * 8 + c) * 512:(wi * 8 + c + 1) * 512],
                            W[c * 128:(c + 1) * 128, c0:c0 + 512], "wx")
            return t

        G = {"xstage_free": [0, 0],
             "pT_free": 0,
             "xlT_free": 0,
             "pP_free": [0, 0],
             "psb_free": [0, 0, 0],
             "pS_free": [0, 0, 0],
             "pC_free": [0, 0],
             "bias_free": 0,
             "rz_free": 0,
             "pZ_free": 0,
             "wosum_free": 0,
             "x2_store": 0,
             "hq_free": 0,
             "w1s_free": [0, 0],
             "w2s_free": [0, 0],
             "xlT2_free": 0,
             "epi_v": 0,
             "ctxst_free": [0, 0],
             "ctxld_free": [0, 0],
             "chain_v": 0,
             "crow_free": 0,
             "zrow_sfree": 0,
             "wstg_free": [0, 0],
             "wstgb_free": [0, 0],
             "pP_sfree": 0}

        PRECAST_UNITS = ([("wo", i) for i in range(H)]
                         + [("w1", i) for i in range(32)]
                         + [("w2", i) for i in range(32)])
        PRE = {"i": 0}

        def precast_emit(n_units):
            for _ in range(n_units):
                if PRE["i"] >= len(PRECAST_UNITS):
                    return
                kind, idx = PRECAST_UNITS[PRE["i"]]
                buf = PRE["i"] % 2
                nm = "wca" if buf == 0 else "wcb"
                snm = "wcsa" if buf == 0 else "wcsb"
                PRE["i"] += 1
                if kind == "w1":
                    src = w1d[:, idx * 128:(idx + 1) * 128].rearrange(
                        "(c p) f -> p c f", p=128)
                    ldst = wstg[buf][:].rearrange("p (c f) -> p c f", c=8)
                    rows = 128
                    sdst = w1bf[:, idx * 1024:(idx + 1) * 1024]
                elif kind == "w2":
                    src = w2d[idx * 128:(idx + 1) * 128, :]
                    ldst = wstg[buf][:]
                    rows = 128
                    sdst = w2bf[:, idx * 1024:(idx + 1) * 1024]
                else:
                    src = Wo[idx * 64:(idx + 1) * 64, :]
                    ldst = wstg[buf][0:64, :]
                    rows = 64
                    sdst = wobf[0:64, idx * 1024:(idx + 1) * 1024]
                wait("sync", "v", G["wstg_free"][buf])
                t_l = dma("sync", ldst, src, nm)
                wait("vector", nm, t_l)
                wait("vector", snm, G["wstgb_free"][buf])
                t_cast = op("vector", lambda io, B=buf, R=rows: io.tensor_copy(
                    out=wstgb[B][0:R, :], in_=wstg[B][0:R, :]), sem="v")
                drain("vector")
                G["wstg_free"][buf] = t_cast
                wait("sync", "v", t_cast)
                G["wstgb_free"][buf] = dma("sync", sdst, wstgb[buf][0:rows, :], snm)

        def stream_block(xsrc, blk_i, wx3_tgt, dst, woff, do_v):
            t_ev_last = 0
            for i in range(4):
                tile = blk_i * 4 + i
                sb = xstage[tile % 2]
                xs = "x0" if tile % 2 == 0 else "x1"
                wait("sync", "v", G["xstage_free"][tile % 2])
                t_x = dma("sync", sb[:], xsrc[tile * 128:(tile + 1) * 128, :], xs)
                wait("vector", xs, t_x)
                ln_stats(sb[:], D)
                t_ap = ln_apply(sb[:], xl[:])
                G["xstage_free"][tile % 2] = t_ap
                wait("tensor", "v", G["pT_free"])
                wait("tensor", "v", t_ap)
                tm = None
                for c in range(8):
                    tm = op("tensor", lambda io, cc=c: io.transpose(
                        pT[:, cc * 128:(cc + 1) * 128],
                        xl[:, cc * 128:(cc + 1) * 128], ident[:]),
                        sem="t" if c == 7 else None)
                wait("vector", "t", G["xlT_free"])
                wait("vector", "t", tm)
                t_ev = op("vector", lambda io, ii=i: io.tensor_copy(
                    out=xlT[:].rearrange("p (c n) -> p c n", c=8)[:, :, ii * 128:(ii + 1) * 128],
                    in_=pT[:].rearrange("p (c n) -> p c n", c=8)), sem="v")
                G["pT_free"] = t_ev
                t_ev_last = t_ev
                if do_v:
                    wait("tensor", "v", t_ev)
                    wait("tensor", "wx", wx3_tgt)
                    wait("tensor", "v", G["pP_free"][1])
                    tm2 = None
                    for c in range(8):
                        tm2 = op("tensor", lambda io, cc=c, ii=i: io.matmul(
                            pP[1][:, 0:512],
                            xlT[:, cc * 512 + ii * 128: cc * 512 + (ii + 1) * 128],
                            Wx3[:, (16 + cc) * 512:(17 + cc) * 512],
                            start=(cc == 0), stop=(cc == 7)),
                            sem="t" if c == 7 else None)
                    wait("vector", "t", tm2)
                    tv = None
                    for hh in range(8):
                        tv = op("vector", lambda io, h=hh, tl=tile: io.tensor_copy(
                            out=Vsb[h][:, tl * 65: tl * 65 + 64],
                            in_=pP[1][:, h * 64:(h + 1) * 64]),
                            sem="v" if hh == 7 else None)
                    G["pP_free"][1] = tv
            base = blk_i * 512
            wait("tensor", "wx", wx3_tgt)
            wait("tensor", "v", t_ev_last)
            for hp in range(4):
                wait("tensor", "v", G["pP_free"][0])
                tm3 = None
                for c in range(8):
                    tm3 = op("tensor", lambda io, cc=c, hhp=hp, wo=woff: io.matmul(
                        pP[0][:, 0:512],
                        Wx3[:, (wo + cc) * 512 + hhp * 128: (wo + cc) * 512 + hhp * 128 + 128],
                        xlT[:, cc * 512:(cc + 1) * 512],
                        start=(cc == 0), stop=(cc == 7)),
                        sem="t" if c == 7 else None)
                wait("vector", "t", tm3)
                tv2 = op("vector", lambda io, hhp=hp, b=base: io.tensor_copy(
                    out=dst[hhp][:, b:b + 512], in_=pP[0][:, 0:512]), sem="v")
                G["pP_free"][0] = tv2
            G["xlT_free"] = cnt["t"]

        def attention_group(hg, t_proj_v):
            wait("tensor", "v", t_proj_v)
            pCs = [pC, pP[1]]
            for h8 in range(8):
                h = hg * 8 + h8
                hp = h8 // 2
                off = (h8 % 2) * 64
                slope = SLOPES[h]
                wait("scalar", "v", G["crow_free"])
                t_crow = op("scalar", lambda io, sl=slope: io.activation(
                    out=crow[64:65, 0:512], in_=qpr[64:65, 0:512],
                    func=AF.Exp, bias=0.0, scale=-sl / 8.0), sem="s")
                for g in range(NG):
                    csel = (h8 * NG + g) % 2
                    pCsel = pCs[csel]
                    wait("vector", "s", G["bias_free"])
                    t_bias = op("vector", lambda io, sl=slope, col=h * NG + g:
                                io.tensor_scalar(
                                    out=biasmat[:, 0:NT], in0=kpos[:, 0:NT],
                                    scalar1=sl / 8.0, scalar2=qshift[:, col:col + 1],
                                    op0=ALU.mult, op1=ALU.add), sem="v")
                    drain("vector")
                    kmax = 8 * g + 8
                    last_exp = 0
                    qk_t = {}

                    def emit_qk(kt):
                        s_lo = max(4 * g, kt // 2)
                        c0 = (s_lo - 4 * g) * 128
                        pb = kt % 3
                        wait("tensor", "s", G["pS_free"][pb])
                        qk_t[kt] = op("tensor",
                                      lambda io, KT=kt, C0=c0, HP=hp, OF=off,
                                      GG=g, PB=pb:
                                      io.matmul(
                                          pS[PB][:, C0:512],
                                          Ksb[HP][OF:OF + 64,
                                                  KT * 128:(KT + 1) * 128],
                                          Qsb[HP][OF:OF + 64,
                                                  GG * 512 + C0:(GG + 1) * 512],
                                          start=True, stop=True), sem="t")

                    emit_qk(0)
                    for kt in range(kmax):
                        s_lo = max(4 * g, kt // 2)
                        c0 = (s_lo - 4 * g) * 128
                        pb = kt % 3
                        if kt + 1 < kmax:
                            emit_qk(kt + 1)
                        wait("scalar", "t", qk_t[kt])
                        wait("scalar", "v", t_bias)
                        wait("scalar", "t", G["psb_free"][pb])
                        t_exp = op("scalar", lambda io, KT=kt, C0=c0, PB=pb:
                                   io.activation(
                                       out=psb[PB][:, C0:512], in_=pS[PB][:, C0:512],
                                       func=AF.Exp, bias=biasmat[:, KT:KT + 1],
                                       scale=0.125), sem="s")
                        G["pS_free"][pb] = t_exp
                        last_exp = t_exp
                        s_star = kt // 2
                        masked = (4 * g <= s_star < 4 * g + 4) and (kt >= 8 * g)
                        if masked:
                            mc = (kt % 2) * 128
                            wait("vector", "s", t_exp)
                            t_mk = op("vector", lambda io, SS=s_star, GG=g, MC=mc, PB=pb:
                                      io.tensor_tensor(
                                          out=psb[PB][:, (SS - 4 * GG) * 128:(SS - 4 * GG + 1) * 128],
                                          in0=psb[PB][:, (SS - 4 * GG) * 128:(SS - 4 * GG + 1) * 128],
                                          in1=masks[:, MC:MC + 128], op=ALU.mult),
                                      sem="v")
                            wait("tensor", "v", t_mk)
                        else:
                            wait("tensor", "s", t_exp)
                        if kt == 0:
                            wait("tensor", "v", G["pC_free"][csel])
                        t_av = op("tensor",
                                  lambda io, KT=kt, C0=c0, HH=h8, PB=pb, KM=kmax,
                                  PCS=pCsel:
                                  io.matmul(
                                      PCS[0:65, C0:512],
                                      Vsb[HH][:, KT * 65:KT * 65 + 65],
                                      psb[PB][:, C0:512],
                                      start=(KT == 0), stop=(KT == KM - 1)), sem="t")
                        G["psb_free"][pb] = t_av
                    t_avl = cnt["t"]
                    wait("vector", "t", t_avl)
                    wait("vector", "t", G["rz_free"])
                    wait("vector", "s", t_crow)
                    wait("vector", "s", G["zrow_sfree"])
                    t_zm = op("vector", lambda io, PCS=pCsel: io.tensor_tensor(
                        out=zrow[64:65, 0:512], in0=PCS[64:65, 0:512],
                        in1=crow[64:65, 0:512], op=ALU.mult), sem="v")
                    drain("vector")
                    wait("scalar", "v", t_zm)
                    op("scalar", lambda io: io.activation(
                        out=zrow[64:65, 0:512], in_=zrow[64:65, 0:512],
                        func=AF.Ln, bias=0.0, scale=1.0))
                    drain("scalar")
                    t_rze = op("scalar", lambda io: io.activation(
                        out=rzrow[64:65, 0:512], in_=zrow[64:65, 0:512],
                        func=AF.Exp, bias=0.0, scale=-1.0), sem="s")
                    wait("vector", "s", t_rze)
                    t_rz = op("vector", lambda io: io.tensor_tensor(
                        out=rzrow[64:65, 0:512], in0=rzrow[64:65, 0:512],
                        in1=crow[64:65, 0:512], op=ALU.mult), sem="v")
                    drain("vector")
                    G["zrow_sfree"] = t_rze
                    wait("tensor", "v", t_rz)
                    wait("tensor", "v", G["pZ_free"])
                    t_bc = op("tensor", lambda io: io.matmul(
                        pZ[0:64, 0:512], onescf[64:65, 0:64], rzrow[64:65, 0:512],
                        start=True, stop=True), sem="t")
                    G["rz_free"] = t_bc
                    wait("vector", "t", t_bc)
                    t_evz = op("vector", lambda io: io.tensor_copy(
                        out=rzbc[:, 0:512], in_=pZ[0:64, 0:512]), sem="v")
                    G["pZ_free"] = t_evz
                    drain("vector")
                    cbuf = (h8 * NG + g) % 2
                    ctx_nm = "ctxa" if cbuf == 0 else "ctxb"
                    wait("vector", ctx_nm, G["ctxst_free"][cbuf])
                    t_nm = op("vector", lambda io, CB=cbuf, PCS=pCsel: io.tensor_tensor(
                        out=ctxst[CB][:, 0:512],
                        in0=PCS[0:64, 0:512], in1=rzbc[:, 0:512], op=ALU.mult),
                        sem="v")
                    drain("vector")
                    wait("gpsimd", "v", t_nm)
                    G["ctxst_free"][cbuf] = dma(
                        "gpsimd", ctx_d[h * 64:(h + 1) * 64, g * 512:(g + 1) * 512],
                        ctxst[cbuf][:, 0:512], ctx_nm)
                    G["pC_free"][csel] = t_nm
                    G["bias_free"] = last_exp
                    G["crow_free"] = t_rz
                    it = hg * 32 + h8 * NG + g
                    precast_emit((it + 1) * 80 // 64 - it * 80 // 64)

        t_wx3 = load_wx3(0)
        for hg in range(2):
            if hg == 1:
                wait("gpsimd", "t", cnt["t"])
                t_wx3 = load_wx3(1)
                wait("vector", "t", cnt["t"])
            for b8 in range(8):
                stream_block(x_full, b8, t_wx3, Ksb, 8, do_v=True)
            for b4 in range(4):
                stream_block(x_my, b4, t_wx3, Qsb, 0, do_v=False)
            attention_group(hg, cnt["v"])
            G["pP_free"] = [cnt["v"], cnt["v"]]
            G["pC_free"] = [cnt["v"], cnt["v"]]

        def barrier():
            snap = dict(cnt)
            for eng in ("sync", "tensor", "vector", "scalar", "gpsimd"):
                for snm, val in snap.items():
                    wait(eng, snm, val)

        pstk1.close()
        stk1.close()
        barrier()

        stk2 = contextlib.ExitStack()
        e2 = stk2.enter_context
        Wosb = e2(nc.sbuf_tensor("Wosbt", [64, H * D], BF16))
        bo_b = e2(nc.sbuf_tensor("bo_bt", [128, D], F32))
        ctxld = [e2(nc.sbuf_tensor(f"ctxldt{i}", [64, 512], BF16)) for i in range(2)]
        pstk2 = contextlib.ExitStack()
        ep2 = pstk2.enter_context
        pW = [ep2(nc.psum_tensor(f"pWp{i}", [128, 512], F32)) for i in range(8)]

        t_wo = None
        for hh in range(H):
            t_wo = dma("scalar", Wosb[0:64, hh * D:(hh + 1) * D],
                       wobf[0:64, hh * 1024:(hh + 1) * 1024], "wo")
        bcast_row(0, bo_b, pW[0])
        t_bo_v = cnt["v"]
        pW_free_v = t_bo_v
        for blk in range(4):
            for hh in range(H):
                cb = hh % 2
                cxs = "cx0" if cb == 0 else "cx1"
                wait("sync", "t", G["ctxld_free"][cb])
                t_cx = dma("sync", ctxld[cb][:, 0:512],
                           ctx_d[hh * 64:(hh + 1) * 64, blk * 512:(blk + 1) * 512],
                           cxs)
                wait("tensor", cxs, t_cx)
                if hh == 0:
                    wait("tensor", "wo", t_wo)
                    wait("tensor", "v", pW_free_v)
                tm = None
                for tl in range(4):
                    for bank in range(2):
                        tm = op("tensor",
                                lambda io, HHH=hh, TL=tl, BK=bank, CB=cb:
                                io.matmul(
                                    pW[TL * 2 + BK][:, 0:512],
                                    ctxld[CB][:, TL * 128:(TL + 1) * 128],
                                    Wosb[0:64, HHH * D + BK * 512: HHH * D + BK * 512 + 512],
                                    start=(HHH == 0), stop=(HHH == H - 1)),
                                sem="t" if (tl == 3 and bank == 1) else None)
                G["ctxld_free"][cb] = tm
            t_womm = cnt["t"]
            for tl in range(4):
                tt = blk * 4 + tl
                sb = xstage[tt % 2]
                xs = "x0" if tt % 2 == 0 else "x1"
                wait("sync", "v", G["xstage_free"][tt % 2])
                t_x = dma("sync", sb[:], x_my[tt * 128:(tt + 1) * 128, :], xs)
                wait("vector", "t", t_womm)
                wait("vector", "v", G["wosum_free"])
                for bank in range(2):
                    op("vector", lambda io, TL=tl, BK=bank: io.scalar_tensor_tensor(
                        out=wosum[:, BK * 512:(BK + 1) * 512],
                        in0=pW[TL * 2 + BK][:, 0:512],
                        scalar=1.0, in1=bo_b[:, BK * 512:(BK + 1) * 512],
                        op0=ALU.mult, op1=ALU.add),
                       sem="v" if bank == 1 else None)
                drain("vector")
                pW_free_v = cnt["v"]
                ln_stats(wosum[:], D)
                t_ap = ln_apply(wosum[:], lntmp[:])
                G["wosum_free"] = t_ap
                wait("vector", xs, t_x)
                t_x2 = op("vector", lambda io, TT=tt: io.tensor_tensor(
                    out=x2t[TT % 2][:], in0=xstage[TT % 2][:], in1=lntmp[:],
                    op=ALU.add), sem="v")
                drain("vector")
                G["xstage_free"][tt % 2] = t_x2
                x2s_nm = "x2a" if tt % 2 == 0 else "x2b"
                wait("scalar", "v", t_x2)
                G["x2_store"] = dma("scalar", x2_d[tt * 128:(tt + 1) * 128, :],
                                    x2t[tt % 2][:], x2s_nm)
                wait("vector", x2s_nm, G["x2_store"])

        pstk2.close()
        stk2.close()
        barrier()

        stk3 = contextlib.ExitStack()
        e3 = stk3.enter_context
        b2_b = e3(nc.sbuf_tensor("b2_bt", [128, D], F32))
        w1s = [e3(nc.sbuf_tensor(f"w1st{i}", [128, 8 * 128], BF16)) for i in range(2)]
        w2s = [e3(nc.sbuf_tensor(f"w2st{i}", [128, D], BF16)) for i in range(2)]
        hq = e3(nc.sbuf_tensor("hqt", [128, 32 * 1024], BF16))
        xlT2 = e3(nc.sbuf_tensor("xlT2t", [128, 8 * 1024], BF16))
        c_u = e3(nc.sbuf_tensor("c_ut", [128, 1024], F32))
        c_ti = e3(nc.sbuf_tensor("c_tit", [128, 1024], I32))
        c_tf = e3(nc.sbuf_tensor("c_tft", [128, 1024], F32))
        c_e = e3(nc.sbuf_tensor("c_et", [128, 1024], F32))
        c_a = e3(nc.sbuf_tensor("c_at", [128, 1024], F32))
        c_nt = e3(nc.sbuf_tensor("c_ntt", [128, 1024], F32))
        c_ti2 = e3(nc.sbuf_tensor("c_ti2t", [128, 1024], I32))
        c_ri = e3(nc.sbuf_tensor("c_rit", [128, 1024], I32))
        c_rf = e3(nc.sbuf_tensor("c_rft", [128, 1024], F32))
        c_z = e3(nc.sbuf_tensor("c_zt", [128, 1024], F32))
        c_g0 = e3(nc.sbuf_tensor("c_g0t", [128, 1024], F32))
        c_g1 = e3(nc.sbuf_tensor("c_g1t", [128, 1024], F32))
        c_y = e3(nc.sbuf_tensor("c_yt", [128, 1024], F32))
        pstk3 = contextlib.ExitStack()
        ep3 = pstk3.enter_context
        pM = [ep3(nc.psum_tensor(f"pMp{i}", [128, 512], F32)) for i in range(4)]
        pT3 = ep3(nc.psum_tensor("pT3p", [128, D], BF16))

        bcast_row(64, b2_b, pM[0])
        G["epi_v"] = cnt["v"]

        LADDER_COLS = {}

        def sign_peel(vbuf, bbuf, levels, tag, width=1024):
            K = 0.0
            hw_ = width // 2
            halves = [(0, hw_), (hw_, width)]
            t_s = [0, 0]
            t_v = [0, 0]
            wait("scalar", "v", cnt["v"])
            for i, c in enumerate(levels):
                col = LADDER_COLS[tag][i]
                for hh in (0, 1):
                    lo, hi = halves[hh]
                    if t_v[hh]:
                        wait("scalar", "v", t_v[hh])
                    t_s[hh] = op("scalar",
                                 lambda io, V=vbuf, Bb=bbuf, CL=col, L=lo, Hh=hi:
                                 io.activation(
                                     out=Bb[:, L:Hh], in_=V[:, L:Hh], func=AF.Sign,
                                     bias=ladc[:, CL:CL + 1], scale=1.0), sem="s")
                for hh in (0, 1):
                    lo, hi = halves[hh]
                    wait("vector", "s", t_s[hh])
                    t_v[hh] = op("vector",
                                 lambda io, V=vbuf, Bb=bbuf, C=c, L=lo, Hh=hi:
                                 io.scalar_tensor_tensor(
                                     out=V[:, L:Hh], in0=Bb[:, L:Hh],
                                     scalar=float(-C / 2.0), in1=V[:, L:Hh],
                                     op0=ALU.mult, op1=ALU.add), sem="v")
                    drain("vector")
                K += -c / 2.0
            wait("scalar", "v", max(t_v))
            return K

        LADDER_COLS.update({
            "hA": [0, 1, 2, 3, 4],
            "hB": [5, 6, 7],
            "h3n": [8, 9, 10, 11],
            "h4n": [12, 13, 14, 15, 16],
        })

        def ffn_chain(ft):
            wait("scalar", "v", G["chain_v"])
            for half in range(2):
                op("scalar", lambda io, HF=half, FT=ft: io.activation(
                    out=c_u[:, HF * 512:(HF + 1) * 512], in_=pM[HF][:, 0:512],
                    func=AF.Identity, bias=b1bias[:, FT:FT + 1], scale=256.0),
                   sem="s" if half == 1 else None)
            t_u = cnt["s"]
            G["pP_sfree"] = t_u
            wait("vector", "s", t_u)
            op("vector", lambda io: io.tensor_scalar(
                out=c_ti[:], in0=c_u[:], scalar1=1.0, scalar2=-16.0,
                op0=ALU.mult, op1=ALU.add))
            drain("vector")
            op("vector", lambda io: io.tensor_copy(out=c_tf[:], in_=c_ti[:]))
            drain("vector")
            op("vector", lambda io: io.scalar_tensor_tensor(
                out=c_e[:], in0=c_tf[:], scalar=-1.0, in1=c_u[:],
                op0=ALU.mult, op1=ALU.add), sem="v")
            drain("vector")
            sign_peel(c_e, c_a, [16, 8, 4, 2, 1], "hA")
            op("vector", lambda io: io.tensor_scalar(
                out=c_ti[:], in0=c_u[:], scalar1=1.0 / 64, scalar2=-4.0,
                op0=ALU.mult, op1=ALU.add))
            drain("vector")
            op("vector", lambda io: io.tensor_copy(out=c_tf[:], in_=c_ti[:]))
            drain("vector")
            op("vector", lambda io: io.scalar_tensor_tensor(
                out=c_rf[:], in0=c_tf[:], scalar=-64.0, in1=c_u[:],
                op0=ALU.mult, op1=ALU.add), sem="v")
            drain("vector")
            sign_peel(c_rf, c_a, [256, 128, 64], "hB")
            t_z = op("vector", lambda io: io.tensor_tensor(
                out=c_z[:], in0=c_rf[:], in1=c_u[:], op=ALU.subtract), sem="v")
            drain("vector")
            op("vector", lambda io: io.tensor_tensor(
                out=c_rf[:], in0=c_rf[:], in1=c_e[:], op=ALU.subtract))
            drain("vector")
            op("vector", lambda io: io.tensor_scalar(
                out=c_rf[:], in0=c_rf[:], scalar1=1.0, scalar2=-208.5,
                op0=ALU.mult, op1=ALU.add))
            drain("vector")
            wait("scalar", "v", t_z)
            op("scalar", lambda io: io.activation(
                out=c_g0[:], in_=c_z[:], func=AF.Gelu, bias=cbias[:, 1:2],
                scale=-1.0 / 256))
            drain("scalar")
            t_g = op("scalar", lambda io: io.activation(
                out=c_g1[:], in_=c_z[:], func=AF.Gelu, bias=cbias[:, 2:3],
                scale=-1.0 / 256), sem="s")
            wait("vector", "s", t_g)
            op("vector", lambda io: io.tensor_tensor(
                out=c_g1[:], in0=c_g1[:], in1=c_g0[:], op=ALU.subtract))
            drain("vector")
            op("vector", lambda io: io.scalar_tensor_tensor(
                out=c_e[:], in0=c_g1[:], scalar=1.0 / 64, in1=c_rf[:],
                op0=ALU.mult, op1=ALU.mult))
            drain("vector")
            t_y = op("vector", lambda io: io.tensor_tensor(
                out=c_y[:], in0=c_e[:], in1=c_g0[:], op=ALU.add), sem="v")
            wait("scalar", "v", t_y)
            t_u2 = op("scalar", lambda io: io.activation(
                out=c_u[:], in_=c_y[:], func=AF.Identity, bias=cbias[:, 3:4],
                scale=256.0), sem="s")
            wait("vector", "s", t_u2)
            op("vector", lambda io: io.tensor_scalar(
                out=c_ti[:], in0=c_u[:], scalar1=1.0, scalar2=-8.0,
                op0=ALU.mult, op1=ALU.add))
            drain("vector")
            op("vector", lambda io: io.tensor_copy(out=c_tf[:], in_=c_ti[:]))
            drain("vector")
            op("vector", lambda io: io.scalar_tensor_tensor(
                out=c_e[:], in0=c_tf[:], scalar=-1.0, in1=c_u[:],
                op0=ALU.mult, op1=ALU.add), sem="v")
            drain("vector")
            sign_peel(c_e, c_a, [8, 4, 2, 1], "h3n")
            t_ca = op("vector", lambda io: io.tensor_tensor(
                out=c_a[:], in0=c_u[:], in1=c_e[:], op=ALU.subtract), sem="v")
            drain("vector")
            wait("scalar", "v", t_ca)
            t_hq = op("scalar", lambda io, FT=ft: io.activation(
                out=hq[:, FT * 1024:(FT + 1) * 1024], in_=c_a[:],
                func=AF.Identity, bias=cbias[:, 5:6], scale=1.0 / 256),
                sem="s")
            G["chain_v"] = t_ca
            return t_hq

        wait("sync", "x2a", cnt["x2a"])
        wait("sync", "x2b", cnt["x2b"])
        for fb in range(2):
            for i in range(8):
                tile = fb * 8 + i
                sb = xstage[i % 2]
                xs = "x0" if i % 2 == 0 else "x1"
                wait("sync", "v", G["xstage_free"][i % 2])
                t_x = dma("sync", sb[:], x2_d[tile * 128:(tile + 1) * 128, :], xs)
                wait("vector", xs, t_x)
                ln_stats(sb[:], D)
                t_ap = ln_apply(sb[:], xl[:])
                G["xstage_free"][i % 2] = t_ap
                wait("tensor", "v", G["pT_free"])
                wait("tensor", "v", t_ap)
                tm = None
                for c in range(8):
                    tm = op("tensor", lambda io, cc=c: io.transpose(
                        pT3[:, cc * 128:(cc + 1) * 128],
                        xl[:, cc * 128:(cc + 1) * 128], ident[:]),
                        sem="t" if c == 7 else None)
                wait("vector", "t", G["xlT2_free"])
                wait("vector", "t", tm)
                t_ev = op("vector", lambda io, ii=i: io.tensor_copy(
                    out=xlT2[:].rearrange("p (c n) -> p c n", c=8)[:, :, ii * 128:(ii + 1) * 128],
                    in_=pT3[:].rearrange("p (c n) -> p c n", c=8)), sem="v")
                G["pT_free"] = t_ev
            t_xlT2 = cnt["v"]
            wait("scalar", "t", G["hq_free"])
            for ft in range(32):
                buf = ft % 2
                w1nm = "w1a" if buf == 0 else "w1b"
                wait("sync", "t", G["w1s_free"][buf])
                t_w1 = dma("sync", w1s[buf][:],
                           w1bf[:, ft * 1024:(ft + 1) * 1024], w1nm)
                wait("tensor", w1nm, t_w1)
                if ft == 0:
                    wait("tensor", "v", t_xlT2)
                    wait("tensor", "v", G["epi_v"])
                wait("tensor", "s", G["pP_sfree"])
                tm = None
                for half in range(2):
                    for c in range(8):
                        tm = op("tensor", lambda io, cc=c, HF=half, BUF=buf: io.matmul(
                            pM[HF][:, 0:512],
                            w1s[BUF][:, cc * 128:(cc + 1) * 128],
                            xlT2[:, cc * 1024 + HF * 512: cc * 1024 + (HF + 1) * 512],
                            start=(cc == 0), stop=(cc == 7)),
                            sem="t" if (c == 7 and half == 1) else None)
                G["w1s_free"][buf] = tm
                wait("scalar", "t", tm)
                ffn_chain(ft)
            t_hq_all = cnt["s"]
            G["xlT2_free"] = cnt["t"]
            for tp in range(4):
                tiles = (fb * 8 + tp * 2, fb * 8 + tp * 2 + 1)
                t_w2mm = None
                for f in range(32):
                    buf = f % 2
                    w2nm = "w2a" if buf == 0 else "w2b"
                    wait("gpsimd", "t", G["w2s_free"][buf])
                    t_w2 = dma("gpsimd", w2s[buf][:],
                               w2bf[:, f * 1024:(f + 1) * 1024], w2nm)
                    wait("tensor", w2nm, t_w2)
                    if f == 0:
                        wait("tensor", "s", t_hq_all)
                        wait("tensor", "s", G["pP_sfree"])
                        wait("tensor", "v", G["epi_v"])
                    tm = None
                    for li in range(2):
                        for bank in range(2):
                            tm = op("tensor",
                                    lambda io, FF_=f, LI=li, BK=bank, TL=tiles[li], FB=fb, BUF=buf:
                                    io.matmul(
                                        pM[LI * 2 + BK][:, 0:512],
                                        hq[:, FF_ * 1024 + (TL - FB * 8) * 128:
                                           FF_ * 1024 + (TL - FB * 8) * 128 + 128],
                                        w2s[BUF][:, BK * 512:(BK + 1) * 512],
                                        start=(FF_ == 0), stop=(FF_ == 31)),
                                    sem="t" if (li == 1 and bank == 1) else None)
                    t_w2mm = tm
                    G["w2s_free"][buf] = tm
                for li in range(2):
                    tile = tiles[li]
                    wait("vector", "t", t_w2mm)
                    wait("vector", "v", G["wosum_free"])
                    for bank in range(2):
                        op("vector", lambda io, LI=li, BK=bank: io.scalar_tensor_tensor(
                            out=wosum[:, BK * 512:(BK + 1) * 512],
                            in0=pM[LI * 2 + BK][:, 0:512],
                            scalar=1.0, in1=b2_b[:, BK * 512:(BK + 1) * 512],
                            op0=ALU.mult, op1=ALU.add),
                           sem="v" if bank == 1 else None)
                    drain("vector")
                    t_o = cnt["v"]
                    if li == 1:
                        G["epi_v"] = t_o
                    wait("scalar", "v", t_o)
                    t_u3 = op("scalar", lambda io: io.activation(
                        out=c_u[:], in_=wosum[:], func=AF.Identity,
                        bias=cbias[:, 4:5], scale=256.0), sem="s")
                    wait("vector", "s", t_u3)
                    op("vector", lambda io: io.tensor_scalar(
                        out=c_ti[:], in0=c_u[:], scalar1=1.0, scalar2=-16.0,
                        op0=ALU.mult, op1=ALU.add))
                    drain("vector")
                    op("vector", lambda io: io.tensor_copy(out=c_tf[:], in_=c_ti[:]))
                    drain("vector")
                    op("vector", lambda io: io.scalar_tensor_tensor(
                        out=c_e[:], in0=c_tf[:], scalar=-1.0, in1=c_u[:],
                        op0=ALU.mult, op1=ALU.add), sem="v")
                    drain("vector")
                    sign_peel(c_e, c_a, [16, 8, 4, 2, 1], "h4n")
                    op("vector", lambda io: io.tensor_tensor(
                        out=c_a[:], in0=c_u[:], in1=c_e[:], op=ALU.subtract))
                    drain("vector")
                    op("vector", lambda io: io.tensor_scalar(
                        out=lntmp[:], in0=c_a[:], scalar1=1.0 / 256,
                        scalar2=-31.939453125, op0=ALU.mult, op1=ALU.add))
                    drain("vector")
                    ln_stats(lntmp[:], D)
                    ln_apply(lntmp[:], wosum[:])
                    sb = xstage[li]
                    xs = "x0" if li == 0 else "x1"
                    wait("sync", "v", G["xstage_free"][li])
                    t_x = dma("sync", sb[:], x2_d[tile * 128:(tile + 1) * 128, :], xs)
                    wait("vector", xs, t_x)
                    t_out = op("vector", lambda io, LI=li: io.tensor_tensor(
                        out=outt[LI][:], in0=xstage[LI][:], in1=wosum[:],
                        op=ALU.add), sem="v")
                    drain("vector")
                    G["xstage_free"][li] = t_out
                    G["wosum_free"] = t_out
                    out_nm = "outa" if li == 0 else "outb"
                    wait("scalar", "v", t_out)
                    t_st = dma("scalar", out_d[tile * 128:(tile + 1) * 128, :],
                               outt[li][:], out_nm)
                    wait("vector", out_nm, t_st)
            G["hq_free"] = cnt["t"]
        wait("gpsimd", "outa", cnt["outa"])
        wait("gpsimd", "outb", cnt["outb"])
        pstk3.close()
        stk3.close()

        @BLOCK.sync
        def _(io):
            for f in ops["sync"]:
                f(io)

        @BLOCK.tensor
        def _(io):
            for f in ops["tensor"]:
                f(io)

        @BLOCK.vector
        def _(io):
            for f in ops["vector"]:
                f(io)

        @BLOCK.scalar
        def _(io):
            for f in ops["scalar"]:
                f(io)

        @BLOCK.gpsimd
        def _(io):
            for f in ops["gpsimd"]:
                f(io)

    return nc


_NC_CACHE = {}


def _constants(par):
    ident = np.eye(128, dtype=ml_dtypes.bfloat16)
    kposc = (np.arange(NT)[None, :] * 128 + np.arange(128)[:, None]).astype(np.float32)
    qshift = np.zeros((128, H * NG), dtype=np.float32)
    for h in range(H):
        for g in range(NG):
            t0 = 128 * (8 * g + par)
            t1 = 128 * (8 * g + 6 + par) + 127
            qref = 0.5 * (t0 + t1)
            qshift[:, h * NG + g] = -MSHIFT - SLOPES[h] * qref / 8.0
    ladc = np.zeros((128, 64), dtype=np.float32)
    col = 0
    for levels in ([16, 8, 4, 2, 1], [256, 128, 64], [8, 4, 2, 1],
                   [16, 8, 4, 2, 1]):
        K = 0.0
        for c in levels:
            ladc[:, col] = -(c + K)
            K += c / 2.0
            col += 1
    qpr = np.zeros((128, 512), dtype=np.float32)
    qpr[64, :] = np.repeat(np.array([-384.0, -128.0, 128.0, 384.0], dtype=np.float32), 128)
    lowtri = (np.arange(128)[:, None] <= np.arange(128)[None, :])
    masks = np.zeros((128, 256), dtype=ml_dtypes.bfloat16)
    if par == 0:
        masks[:, 0:128] = lowtri.astype(ml_dtypes.bfloat16)
        masks[:, 128:256] = 0
    else:
        masks[:, 0:128] = 1
        masks[:, 128:256] = lowtri.astype(ml_dtypes.bfloat16)
    return ident, kposc, qshift, masks, qpr, ladc


def kernel(L0_attn_x, L0_attn_Wq, L0_attn_Wk, L0_attn_Wv, L0_attn_Wo, L0_attn_bo,
           L0_ffn_w1, L0_ffn_b1, L0_ffn_w2, L0_ffn_b2):
    x = np.ascontiguousarray(np.asarray(L0_attn_x, dtype=np.float32))
    Wq = np.ascontiguousarray(np.asarray(L0_attn_Wq, dtype=np.float32))
    Wk = np.ascontiguousarray(np.asarray(L0_attn_Wk, dtype=np.float32))
    Wv = np.ascontiguousarray(np.asarray(L0_attn_Wv, dtype=np.float32))
    Wo = np.ascontiguousarray(np.asarray(L0_attn_Wo, dtype=np.float32))
    bo = np.ascontiguousarray(np.asarray(L0_attn_bo, dtype=np.float32))
    w1 = np.ascontiguousarray(np.asarray(L0_ffn_w1, dtype=np.float32))
    b1 = np.ascontiguousarray(
        np.asarray(L0_ffn_b1, dtype=np.float32).reshape(FF // 128, 128).T)
    w2 = np.ascontiguousarray(np.asarray(L0_ffn_w2, dtype=np.float32))
    b2 = np.ascontiguousarray(np.asarray(L0_ffn_b2, dtype=np.float32))

    if "nc" not in _NC_CACHE:
        _NC_CACHE["nc"] = build_nc()
    nc = _NC_CACHE["nc"]

    in_maps = []
    for c in range(8):
        b, par = c // 2, c % 2
        xb = x[b]
        x_my = np.ascontiguousarray(
            xb.reshape(NT, 128, D)[par::2].reshape(MYN, D))
        ident, kposc, qshift, masks, qpr, ladc = _constants(par)
        in_maps.append({
            "x_full": xb, "x_my": x_my,
            "Wq": Wq, "Wk": Wk, "Wv": Wv, "Wo": Wo, "bo": bo,
            "w1": w1, "b1": b1, "w2": w2, "b2": b2,
            "identc": ident, "kposc": kposc, "qshiftc": qshift, "masksc": masks,
            "qprc": qpr, "ladc": ladc,
        })
    res = run_bass_kernel_spmd(nc, in_maps, core_ids=list(range(8)))
    out = np.empty((B, N, D), dtype=np.float32)
    for c in range(8):
        b, par = c // 2, c % 2
        o = res.results[c]["out"].reshape(MYT, 128, D)
        out[b].reshape(NT, 128, D)[par::2] = o
    return out


# version 11 (final)
# speedup vs baseline: 1.2198x; 1.0342x over previous
import contextlib

import numpy as np
import ml_dtypes

import concourse.bass as bass
import concourse.mybir as mybir
from concourse.bass_utils import run_bass_kernel_spmd

F32 = mybir.dt.float32
BF16 = mybir.dt.bfloat16
I32 = mybir.dt.int32
AF = mybir.ActivationFunctionType
ALU = mybir.AluOpType
AX = mybir.AxisListType

B, N, D = 4, 4096, 1024
H = 16
FF = 4096
NT = N // 128
MYT = NT // 2
MYN = MYT * 128
NG = 4
SLOPES = [2.0 ** (-0.5 * (i + 1)) for i in range(H)]
MSHIFT = 12.0


def build_nc():
    nc = bass.Bass("TRN2", target_bir_lowering=False)
    ap = lambda nm, shp, ty, kind: nc.dram_tensor(nm, shp, ty, kind=kind).ap()

    x_full = ap("x_full", [N, D], F32, "ExternalInput")
    x_my = ap("x_my", [MYN, D], F32, "ExternalInput")
    Wq = ap("Wq", [D, D], F32, "ExternalInput")
    Wk = ap("Wk", [D, D], F32, "ExternalInput")
    Wv = ap("Wv", [D, D], F32, "ExternalInput")
    Wo = ap("Wo", [D, D], F32, "ExternalInput")
    bo = ap("bo", [D], F32, "ExternalInput")
    w1d = ap("w1", [D, FF], F32, "ExternalInput")
    b1d = ap("b1", [128, FF // 128], F32, "ExternalInput")
    w2d = ap("w2", [FF, D], F32, "ExternalInput")
    b2d = ap("b2", [D], F32, "ExternalInput")
    ident_in = ap("identc", [128, 128], BF16, "ExternalInput")
    kpos_in = ap("kposc", [128, NT], F32, "ExternalInput")
    qshift_in = ap("qshiftc", [128, H * NG], F32, "ExternalInput")
    masks_in = ap("masksc", [128, 2 * 128], BF16, "ExternalInput")
    qpr_in = ap("qprc", [128, 512], F32, "ExternalInput")
    ladc_in = ap("ladc", [128, 64], F32, "ExternalInput")
    out_d = ap("out", [MYN, D], F32, "ExternalOutput")
    x2_d = ap("x2scr", [MYN, D], F32, "Internal")
    ctx_d = ap("ctxscr", [H * 64, MYN], BF16, "Internal")
    w1bf = ap("w1bf", [128, 32 * 1024], BF16, "Internal")
    w2bf = ap("w2bf", [128, 32 * 1024], BF16, "Internal")
    wobf = ap("wobf", [64, 16 * 1024], BF16, "Internal")

    ops = {e: [] for e in ("sync", "tensor", "vector", "scalar", "gpsimd")}
    cnt = {s: 0 for s in ("t", "v", "s", "cl", "wx", "wo",
                          "x0", "x1", "cx0", "cx1", "w1a", "w1b", "w2a", "w2b",
                          "x2a", "x2b", "outa", "outb", "ctxa", "ctxb",
                          "wca", "wcb", "wcsa", "wcsb")}
    semh = {}

    def op(eng, fn, sem=None):
        if sem is None:
            ops[eng].append(fn)
            return None
        cnt[sem] += 1
        tgt = cnt[sem]
        ops[eng].append(lambda io, f=fn, ss=sem: f(io).then_inc(semh[ss], 1))
        return tgt

    def dma(eng, dst, src, sem):
        cnt[sem] += 16
        tgt = cnt[sem]
        ops[eng].append(lambda io, d=dst, s=src, ss=sem:
                        io.dma_start(out=d, in_=s).then_inc(semh[ss], 16))
        return tgt

    def wait(eng, sem, tgt):
        if tgt:
            ops[eng].append(lambda io, ss=sem, t=tgt: io.wait_ge(semh[ss], t))

    def drain(eng):
        ops[eng].append(lambda io: io.drain())

    stk = contextlib.ExitStack()
    e = stk.enter_context
    with stk:
        ident = e(nc.sbuf_tensor("identt", [128, 128], BF16))
        kpos = e(nc.sbuf_tensor("kpost", [128, NT], F32))
        qshift = e(nc.sbuf_tensor("qshiftt", [128, H * NG], F32))
        masks = e(nc.sbuf_tensor("maskst", [128, 2 * 128], BF16))
        qpr = e(nc.sbuf_tensor("qprt", [128, 512], F32))
        ladc = e(nc.sbuf_tensor("ladct", [128, 64], F32))
        crow = e(nc.sbuf_tensor("crowt", [128, 512], F32))
        biasmat = e(nc.sbuf_tensor("biasmatt", [128, NT], F32))
        biasmat2 = e(nc.sbuf_tensor("biasmat2t", [128, NT], F32))
        b1bias = e(nc.sbuf_tensor("b1biast", [128, FF // 128], F32))
        brow = e(nc.sbuf_tensor("browt", [128, D], F32))
        onesc = e(nc.sbuf_tensor("onesct", [128, 128], BF16))
        onescf = e(nc.sbuf_tensor("onescft", [128, 128], F32))
        cbias = e(nc.sbuf_tensor("cbiast", [128, 8], F32))
        xstage = [e(nc.sbuf_tensor(f"xstaget{i}", [128, D], F32)) for i in range(2)]
        xl = e(nc.sbuf_tensor("xlt", [128, D], BF16))
        stats = e(nc.sbuf_tensor("statst", [128, 8], F32))
        sqscr = e(nc.sbuf_tensor("sqscrt", [128, D], F32))
        wosum = e(nc.sbuf_tensor("wosumt", [128, D], F32))
        lntmp = e(nc.sbuf_tensor("lntmpt", [128, D], F32))
        x2t = [e(nc.sbuf_tensor(f"x2tt{i}", [128, D], F32)) for i in range(2)]
        outt = [e(nc.sbuf_tensor(f"outtt{i}", [128, D], F32)) for i in range(2)]

        stk1 = contextlib.ExitStack()
        e1 = stk1.enter_context
        Ksb = [e1(nc.sbuf_tensor(f"Ksb{i}", [128, N], BF16)) for i in range(4)]
        Qsb = [e1(nc.sbuf_tensor(f"Qsb{i}", [128, MYN], BF16)) for i in range(4)]
        Vsb = [e1(nc.sbuf_tensor(f"Vsb{i}", [128, NT * 65], BF16)) for i in range(8)]
        Wx3 = e1(nc.sbuf_tensor("Wx3t", [128, 24 * 512], BF16))
        xlT = e1(nc.sbuf_tensor("xlTt", [128, 8 * 512], BF16))
        psb = [e1(nc.sbuf_tensor(f"psbt{i}", [128, 512], BF16)) for i in range(3)]
        zrow = e1(nc.sbuf_tensor("zrowt", [128, 512], F32))
        rzrow = e1(nc.sbuf_tensor("rzrowt", [128, 512], F32))
        rzbc = e1(nc.sbuf_tensor("rzbct", [64, 512], F32))
        ctxst = [e1(nc.sbuf_tensor(f"ctxstt{i}", [64, 512], BF16)) for i in range(2)]
        wstg = [e1(nc.sbuf_tensor(f"wstgt{i}", [128, 1024], F32)) for i in range(2)]
        wstgb = [e1(nc.sbuf_tensor(f"wstgbt{i}", [128, 1024], BF16)) for i in range(2)]

        pstk1 = contextlib.ExitStack()
        ep1 = pstk1.enter_context
        pS = [ep1(nc.psum_tensor(f"pSp{i}", [128, 512], F32)) for i in range(3)]
        pC = ep1(nc.psum_tensor("pCp", [65, 512], F32))
        pZ = ep1(nc.psum_tensor("pZp", [64, 512], F32))
        pT = ep1(nc.psum_tensor("pTp", [128, D], BF16))
        pP = [ep1(nc.psum_tensor(f"pPp{i}", [128, 512], F32)) for i in range(2)]

        for snm in cnt:
            semh[snm] = e(nc.semaphore(f"sem_{snm}"))
        BLOCK = e(nc.Block())

        t_c = 0
        for dst, src in ((ident, ident_in), (kpos, kpos_in), (qshift, qshift_in),
                         (masks, masks_in), (qpr, qpr_in), (ladc, ladc_in)):
            t_c = dma("sync", dst[:], src, "cl")
        t_c = dma("sync", brow[0:1, :], bo.rearrange("(o d) -> o d", o=1), "cl")
        t_c = dma("sync", brow[64:65, :], b2d.rearrange("(o d) -> o d", o=1), "cl")
        t_c = dma("sync", b1bias[:], b1d, "cl")
        t_consts = t_c

        wait("vector", "cl", t_consts)
        op("vector", lambda io: io.memset(onesc[:], 1.0))
        op("vector", lambda io: io.memset(onescf[:], 1.0))
        for col, val in ((0, 1e-5), (1, -7.125), (2, -6.875), (3, 64.5),
                         (4, 8192.5), (5, -0.220703125)):
            op("vector", lambda io, C=col, VV=val: io.memset(cbias[:, C:C + 1], VV))
        for hh in range(8):
            op("vector", lambda io, h=hh: io.memset(
                Vsb[h][:].rearrange("p (k c) -> p k c", c=65)[:, :, 64:65], 1.0))
        t_b1b = op("vector", lambda io: io.tensor_scalar(
            out=b1bias[:], in0=b1bias[:], scalar1=256.0, scalar2=2048.5,
            op0=ALU.mult, op1=ALU.add), sem="v")
        drain("vector")

        def bcast_row(r_i, dstt, pbank):
            for half in range(2):
                for prange in range(2):
                    t_bc = op("tensor", lambda io, rr=r_i, hh=half, pr=prange, pb=pbank:
                              io.matmul(pb[0:64, 0:512],
                                        onescf[rr:rr + 1, pr * 64:(pr + 1) * 64],
                                        brow[rr:rr + 1, hh * 512:(hh + 1) * 512],
                                        start=True, stop=True,
                                        tile_position=(rr, 0)), sem="t")
                    wait("vector", "t", t_bc)
                    t_bcv = op("vector", lambda io, d=dstt, hh=half, pr=prange, pb=pbank:
                               io.tensor_copy(
                                   out=d[pr * 64:(pr + 1) * 64, hh * 512:(hh + 1) * 512],
                                   in_=pb[0:64, 0:512]), sem="v")
                    wait("tensor", "v", t_bcv)

        ln_guard = {"s": 0}

        def ln_stats(src_ap, Dd):
            wait("vector", "s", ln_guard["s"])
            op("vector", lambda io, sa=src_ap: io.tensor_reduce(
                out=stats[:, 0:1], in_=sa, axis=AX.X, op=ALU.add))
            drain("vector")
            t2 = op("vector", lambda io, dd=Dd: io.tensor_scalar(
                out=stats[:, 0:1], in0=stats[:, 0:1], scalar1=-1.0 / dd, scalar2=0.0,
                op0=ALU.mult, op1=ALU.add), sem="v")
            drain("vector")
            wait("scalar", "v", t2)
            op("scalar", lambda io, sa=src_ap, dd=Dd: io.activation(
                out=sqscr[:, 0:dd], in_=sa, func=AF.Square,
                bias=stats[:, 0:1], scale=1.0, accum_out=stats[:, 2:3]))
            drain("scalar")
            t4 = op("scalar", lambda io, dd=Dd: io.activation(
                out=stats[:, 3:4], in_=stats[:, 2:3], func=AF.Sqrt,
                bias=cbias[:, 0:1], scale=1.0 / dd), sem="s")
            wait("vector", "s", t4)
            t5 = op("vector", lambda io: io.reciprocal(
                out=stats[:, 1:2], in_=stats[:, 3:4]), sem="v")
            drain("vector")
            ln_guard["s"] = t4
            return t5

        def ln_apply(src_ap, dst_ap):
            t = op("vector", lambda io, sa=src_ap, da=dst_ap: io.tensor_scalar(
                out=da, in0=sa, scalar1=stats[:, 0:1], scalar2=stats[:, 1:2],
                op0=ALU.add, op1=ALU.mult), sem="v")
            drain("vector")
            return t

        def load_wx3(hg):
            c0 = hg * 512
            t = None
            for wi, W in enumerate((Wq, Wk, Wv)):
                for c in range(8):
                    t = dma("gpsimd",
                            Wx3[:, (wi * 8 + c) * 512:(wi * 8 + c + 1) * 512],
                            W[c * 128:(c + 1) * 128, c0:c0 + 512], "wx")
            return t

        G = {"xstage_free": [0, 0],
             "pT_free": 0,
             "xlT_free": 0,
             "pP_free": [0, 0],
             "psb_free": [0, 0, 0],
             "pS_free": [0, 0, 0],
             "pC_free": [0, 0],
             "bias_free": [0, 0],
             "rz_free": 0,
             "pZ_free": 0,
             "wosum_free": 0,
             "x2_store": 0,
             "hq_free": 0,
             "w1s_free": [0, 0],
             "w2s_free": [0, 0],
             "xlT2_free": 0,
             "epi_v": 0,
             "ctxst_free": [0, 0],
             "ctxld_free": [0, 0],
             "chain_v": 0,
             "crow_free": 0,
             "zrow_sfree": 0,
             "wstg_free": [0, 0],
             "wstgb_free": [0, 0],
             "pP_sfree": 0}

        PRECAST_UNITS = ([("wo", i) for i in range(H)]
                         + [("w1", i) for i in range(32)]
                         + [("w2", i) for i in range(32)])
        PRE = {"i": 0}

        def precast_emit(n_units):
            for _ in range(n_units):
                if PRE["i"] >= len(PRECAST_UNITS):
                    return
                kind, idx = PRECAST_UNITS[PRE["i"]]
                buf = PRE["i"] % 2
                nm = "wca" if buf == 0 else "wcb"
                snm = "wcsa" if buf == 0 else "wcsb"
                PRE["i"] += 1
                if kind == "w1":
                    src = w1d[:, idx * 128:(idx + 1) * 128].rearrange(
                        "(c p) f -> p c f", p=128)
                    ldst = wstg[buf][:].rearrange("p (c f) -> p c f", c=8)
                    rows = 128
                    sdst = w1bf[:, idx * 1024:(idx + 1) * 1024]
                elif kind == "w2":
                    src = w2d[idx * 128:(idx + 1) * 128, :]
                    ldst = wstg[buf][:]
                    rows = 128
                    sdst = w2bf[:, idx * 1024:(idx + 1) * 1024]
                else:
                    src = Wo[idx * 64:(idx + 1) * 64, :]
                    ldst = wstg[buf][0:64, :]
                    rows = 64
                    sdst = wobf[0:64, idx * 1024:(idx + 1) * 1024]
                wait("sync", "v", G["wstg_free"][buf])
                t_l = dma("sync", ldst, src, nm)
                wait("vector", nm, t_l)
                wait("vector", snm, G["wstgb_free"][buf])
                t_cast = op("vector", lambda io, B=buf, R=rows: io.tensor_copy(
                    out=wstgb[B][0:R, :], in_=wstg[B][0:R, :]), sem="v")
                drain("vector")
                G["wstg_free"][buf] = t_cast
                wait("sync", "v", t_cast)
                G["wstgb_free"][buf] = dma("sync", sdst, wstgb[buf][0:rows, :], snm)

        def stream_block(xsrc, blk_i, wx3_tgt, dst, woff, do_v):
            t_ev_last = 0
            for i in range(4):
                tile = blk_i * 4 + i
                sb = xstage[tile % 2]
                xs = "x0" if tile % 2 == 0 else "x1"
                wait("sync", "v", G["xstage_free"][tile % 2])
                t_x = dma("sync", sb[:], xsrc[tile * 128:(tile + 1) * 128, :], xs)
                wait("vector", xs, t_x)
                ln_stats(sb[:], D)
                t_ap = ln_apply(sb[:], xl[:])
                G["xstage_free"][tile % 2] = t_ap
                wait("tensor", "v", G["pT_free"])
                wait("tensor", "v", t_ap)
                tm = None
                for c in range(8):
                    tm = op("tensor", lambda io, cc=c: io.transpose(
                        pT[:, cc * 128:(cc + 1) * 128],
                        xl[:, cc * 128:(cc + 1) * 128], ident[:]),
                        sem="t" if c == 7 else None)
                wait("vector", "t", G["xlT_free"])
                wait("vector", "t", tm)
                t_ev = op("vector", lambda io, ii=i: io.tensor_copy(
                    out=xlT[:].rearrange("p (c n) -> p c n", c=8)[:, :, ii * 128:(ii + 1) * 128],
                    in_=pT[:].rearrange("p (c n) -> p c n", c=8)), sem="v")
                G["pT_free"] = t_ev
                t_ev_last = t_ev
                if do_v:
                    wait("tensor", "v", t_ev)
                    wait("tensor", "wx", wx3_tgt)
                    wait("tensor", "v", G["pP_free"][1])
                    tm2 = None
                    for c in range(8):
                        tm2 = op("tensor", lambda io, cc=c, ii=i: io.matmul(
                            pP[1][:, 0:512],
                            xlT[:, cc * 512 + ii * 128: cc * 512 + (ii + 1) * 128],
                            Wx3[:, (16 + cc) * 512:(17 + cc) * 512],
                            start=(cc == 0), stop=(cc == 7)),
                            sem="t" if c == 7 else None)
                    wait("vector", "t", tm2)
                    tv = None
                    for hh in range(8):
                        tv = op("vector", lambda io, h=hh, tl=tile: io.tensor_copy(
                            out=Vsb[h][:, tl * 65: tl * 65 + 64],
                            in_=pP[1][:, h * 64:(h + 1) * 64]),
                            sem="v" if hh == 7 else None)
                    G["pP_free"][1] = tv
            base = blk_i * 512
            wait("tensor", "wx", wx3_tgt)
            wait("tensor", "v", t_ev_last)
            for hp in range(4):
                wait("tensor", "v", G["pP_free"][0])
                tm3 = None
                for c in range(8):
                    tm3 = op("tensor", lambda io, cc=c, hhp=hp, wo=woff: io.matmul(
                        pP[0][:, 0:512],
                        Wx3[:, (wo + cc) * 512 + hhp * 128: (wo + cc) * 512 + hhp * 128 + 128],
                        xlT[:, cc * 512:(cc + 1) * 512],
                        start=(cc == 0), stop=(cc == 7)),
                        sem="t" if c == 7 else None)
                wait("vector", "t", tm3)
                tv2 = op("vector", lambda io, hhp=hp, b=base: io.tensor_copy(
                    out=dst[hhp][:, b:b + 512], in_=pP[0][:, 0:512]), sem="v")
                G["pP_free"][0] = tv2
            G["xlT_free"] = cnt["t"]

        def attention_group(hg, t_proj_v):
            wait("tensor", "v", t_proj_v)
            pCs = [pC, pP[1]]
            bias_bufs = [biasmat, biasmat2]

            def emit_bias(idx):
                h_ = hg * 8 + idx // NG
                g_ = idx % NG
                bsel_ = idx % 2
                wait("vector", "s", G["bias_free"][bsel_])
                t = op("vector", lambda io, sl=SLOPES[h_], col=h_ * NG + g_,
                       BM=bias_bufs[bsel_]:
                       io.tensor_scalar(
                           out=BM[:, 0:NT], in0=kpos[:, 0:NT],
                           scalar1=sl / 8.0, scalar2=qshift[:, col:col + 1],
                           op0=ALU.mult, op1=ALU.add), sem="v")
                drain("vector")
                return t
            pend = {}
            for h8 in range(8):
                h = hg * 8 + h8
                hp = h8 // 2
                off = (h8 % 2) * 64
                slope = SLOPES[h]
                wait("scalar", "v", G["crow_free"])
                t_crow = op("scalar", lambda io, sl=slope: io.activation(
                    out=crow[64:65, 0:512], in_=qpr[64:65, 0:512],
                    func=AF.Exp, bias=0.0, scale=-sl / 8.0), sem="s")
                for g in range(NG):
                    gidx = h8 * NG + g
                    csel = gidx % 2
                    pCsel = pCs[csel]
                    bsel = gidx % 2
                    bmat = bias_bufs[bsel]
                    t_bias = pend.pop(gidx, None)
                    if t_bias is None:
                        t_bias = emit_bias(gidx)
                    kmax = 8 * g + 8
                    last_exp = 0
                    qk_t = {}

                    def emit_qk(kt):
                        s_lo = max(4 * g, kt // 2)
                        c0 = (s_lo - 4 * g) * 128
                        pb = kt % 3
                        wait("tensor", "s", G["pS_free"][pb])
                        qk_t[kt] = op("tensor",
                                      lambda io, KT=kt, C0=c0, HP=hp, OF=off,
                                      GG=g, PB=pb:
                                      io.matmul(
                                          pS[PB][:, C0:512],
                                          Ksb[HP][OF:OF + 64,
                                                  KT * 128:(KT + 1) * 128],
                                          Qsb[HP][OF:OF + 64,
                                                  GG * 512 + C0:(GG + 1) * 512],
                                          start=True, stop=True), sem="t")

                    emit_qk(0)
                    for kt in range(kmax):
                        s_lo = max(4 * g, kt // 2)
                        c0 = (s_lo - 4 * g) * 128
                        pb = kt % 3
                        if kt + 1 < kmax:
                            emit_qk(kt + 1)
                        wait("scalar", "t", qk_t[kt])
                        wait("scalar", "v", t_bias)
                        wait("scalar", "t", G["psb_free"][pb])
                        t_exp = op("scalar", lambda io, KT=kt, C0=c0, PB=pb,
                                   BM=bmat:
                                   io.activation(
                                       out=psb[PB][:, C0:512], in_=pS[PB][:, C0:512],
                                       func=AF.Exp, bias=BM[:, KT:KT + 1],
                                       scale=0.125), sem="s")
                        G["pS_free"][pb] = t_exp
                        last_exp = t_exp
                        s_star = kt // 2
                        masked = (4 * g <= s_star < 4 * g + 4) and (kt >= 8 * g)
                        if masked:
                            mc = (kt % 2) * 128
                            wait("vector", "s", t_exp)
                            t_mk = op("vector", lambda io, SS=s_star, GG=g, MC=mc, PB=pb:
                                      io.tensor_tensor(
                                          out=psb[PB][:, (SS - 4 * GG) * 128:(SS - 4 * GG + 1) * 128],
                                          in0=psb[PB][:, (SS - 4 * GG) * 128:(SS - 4 * GG + 1) * 128],
                                          in1=masks[:, MC:MC + 128], op=ALU.mult),
                                      sem="v")
                            wait("tensor", "v", t_mk)
                        else:
                            wait("tensor", "s", t_exp)
                        if kt == 0:
                            wait("tensor", "v", G["pC_free"][csel])
                        t_av = op("tensor",
                                  lambda io, KT=kt, C0=c0, HH=h8, PB=pb, KM=kmax,
                                  PCS=pCsel:
                                  io.matmul(
                                      PCS[0:65, C0:512],
                                      Vsb[HH][:, KT * 65:KT * 65 + 65],
                                      psb[PB][:, C0:512],
                                      start=(KT == 0), stop=(KT == KM - 1)), sem="t")
                        G["psb_free"][pb] = t_av
                    G["bias_free"][bsel] = last_exp
                    if gidx + 1 < 32:
                        pend[gidx + 1] = emit_bias(gidx + 1)
                    t_avl = cnt["t"]
                    wait("vector", "t", t_avl)
                    wait("vector", "t", G["rz_free"])
                    wait("vector", "s", t_crow)
                    wait("vector", "s", G["zrow_sfree"])
                    t_zm = op("vector", lambda io, PCS=pCsel: io.tensor_tensor(
                        out=zrow[64:65, 0:512], in0=PCS[64:65, 0:512],
                        in1=crow[64:65, 0:512], op=ALU.mult), sem="v")
                    drain("vector")
                    wait("scalar", "v", t_zm)
                    op("scalar", lambda io: io.activation(
                        out=zrow[64:65, 0:512], in_=zrow[64:65, 0:512],
                        func=AF.Ln, bias=0.0, scale=1.0))
                    drain("scalar")
                    t_rze = op("scalar", lambda io: io.activation(
                        out=rzrow[64:65, 0:512], in_=zrow[64:65, 0:512],
                        func=AF.Exp, bias=0.0, scale=-1.0), sem="s")
                    wait("vector", "s", t_rze)
                    t_rz = op("vector", lambda io: io.tensor_tensor(
                        out=rzrow[64:65, 0:512], in0=rzrow[64:65, 0:512],
                        in1=crow[64:65, 0:512], op=ALU.mult), sem="v")
                    drain("vector")
                    G["zrow_sfree"] = t_rze
                    wait("tensor", "v", t_rz)
                    wait("tensor", "v", G["pZ_free"])
                    t_bc = op("tensor", lambda io: io.matmul(
                        pZ[0:64, 0:512], onescf[64:65, 0:64], rzrow[64:65, 0:512],
                        start=True, stop=True), sem="t")
                    G["rz_free"] = t_bc
                    wait("vector", "t", t_bc)
                    t_evz = op("vector", lambda io: io.tensor_copy(
                        out=rzbc[:, 0:512], in_=pZ[0:64, 0:512]), sem="v")
                    G["pZ_free"] = t_evz
                    drain("vector")
                    cbuf = (h8 * NG + g) % 2
                    ctx_nm = "ctxa" if cbuf == 0 else "ctxb"
                    wait("vector", ctx_nm, G["ctxst_free"][cbuf])
                    t_nm = op("vector", lambda io, CB=cbuf, PCS=pCsel: io.tensor_tensor(
                        out=ctxst[CB][:, 0:512],
                        in0=PCS[0:64, 0:512], in1=rzbc[:, 0:512], op=ALU.mult),
                        sem="v")
                    drain("vector")
                    wait("gpsimd", "v", t_nm)
                    G["ctxst_free"][cbuf] = dma(
                        "gpsimd", ctx_d[h * 64:(h + 1) * 64, g * 512:(g + 1) * 512],
                        ctxst[cbuf][:, 0:512], ctx_nm)
                    G["pC_free"][csel] = t_nm
                    G["crow_free"] = t_rz
                    it = hg * 32 + h8 * NG + g
                    precast_emit((it + 1) * 80 // 64 - it * 80 // 64)

        t_wx3 = load_wx3(0)
        for hg in range(2):
            if hg == 1:
                wait("gpsimd", "t", cnt["t"])
                t_wx3 = load_wx3(1)
                wait("vector", "t", cnt["t"])
            for b8 in range(8):
                stream_block(x_full, b8, t_wx3, Ksb, 8, do_v=True)
            for b4 in range(4):
                stream_block(x_my, b4, t_wx3, Qsb, 0, do_v=False)
            attention_group(hg, cnt["v"])
            G["pP_free"] = [cnt["v"], cnt["v"]]
            G["pC_free"] = [cnt["v"], cnt["v"]]

        def barrier():
            snap = dict(cnt)
            for eng in ("sync", "tensor", "vector", "scalar", "gpsimd"):
                for snm, val in snap.items():
                    wait(eng, snm, val)

        pstk1.close()
        stk1.close()
        barrier()

        stk2 = contextlib.ExitStack()
        e2 = stk2.enter_context
        Wosb = e2(nc.sbuf_tensor("Wosbt", [64, H * D], BF16))
        bo_b = e2(nc.sbuf_tensor("bo_bt", [128, D], F32))
        ctxld = [e2(nc.sbuf_tensor(f"ctxldt{i}", [64, 512], BF16)) for i in range(2)]
        pstk2 = contextlib.ExitStack()
        ep2 = pstk2.enter_context
        pW = [ep2(nc.psum_tensor(f"pWp{i}", [128, 512], F32)) for i in range(8)]

        t_wo = None
        for hh in range(H):
            t_wo = dma("scalar", Wosb[0:64, hh * D:(hh + 1) * D],
                       wobf[0:64, hh * 1024:(hh + 1) * 1024], "wo")
        bcast_row(0, bo_b, pW[0])
        t_bo_v = cnt["v"]
        pW_free_v = t_bo_v
        for blk in range(4):
            for hh in range(H):
                cb = hh % 2
                cxs = "cx0" if cb == 0 else "cx1"
                wait("sync", "t", G["ctxld_free"][cb])
                t_cx = dma("sync", ctxld[cb][:, 0:512],
                           ctx_d[hh * 64:(hh + 1) * 64, blk * 512:(blk + 1) * 512],
                           cxs)
                wait("tensor", cxs, t_cx)
                if hh == 0:
                    wait("tensor", "wo", t_wo)
                    wait("tensor", "v", pW_free_v)
                tm = None
                for tl in range(4):
                    for bank in range(2):
                        tm = op("tensor",
                                lambda io, HHH=hh, TL=tl, BK=bank, CB=cb:
                                io.matmul(
                                    pW[TL * 2 + BK][:, 0:512],
                                    ctxld[CB][:, TL * 128:(TL + 1) * 128],
                                    Wosb[0:64, HHH * D + BK * 512: HHH * D + BK * 512 + 512],
                                    start=(HHH == 0), stop=(HHH == H - 1)),
                                sem="t" if (tl == 3 and bank == 1) else None)
                G["ctxld_free"][cb] = tm
            t_womm = cnt["t"]
            for tl in range(4):
                tt = blk * 4 + tl
                sb = xstage[tt % 2]
                xs = "x0" if tt % 2 == 0 else "x1"
                wait("sync", "v", G["xstage_free"][tt % 2])
                t_x = dma("sync", sb[:], x_my[tt * 128:(tt + 1) * 128, :], xs)
                wait("vector", "t", t_womm)
                wait("vector", "v", G["wosum_free"])
                for bank in range(2):
                    op("vector", lambda io, TL=tl, BK=bank: io.scalar_tensor_tensor(
                        out=wosum[:, BK * 512:(BK + 1) * 512],
                        in0=pW[TL * 2 + BK][:, 0:512],
                        scalar=1.0, in1=bo_b[:, BK * 512:(BK + 1) * 512],
                        op0=ALU.mult, op1=ALU.add),
                       sem="v" if bank == 1 else None)
                drain("vector")
                pW_free_v = cnt["v"]
                ln_stats(wosum[:], D)
                t_ap = ln_apply(wosum[:], lntmp[:])
                G["wosum_free"] = t_ap
                wait("vector", xs, t_x)
                t_x2 = op("vector", lambda io, TT=tt: io.tensor_tensor(
                    out=x2t[TT % 2][:], in0=xstage[TT % 2][:], in1=lntmp[:],
                    op=ALU.add), sem="v")
                drain("vector")
                G["xstage_free"][tt % 2] = t_x2
                x2s_nm = "x2a" if tt % 2 == 0 else "x2b"
                wait("scalar", "v", t_x2)
                G["x2_store"] = dma("scalar", x2_d[tt * 128:(tt + 1) * 128, :],
                                    x2t[tt % 2][:], x2s_nm)
                wait("vector", x2s_nm, G["x2_store"])

        pstk2.close()
        stk2.close()
        barrier()

        stk3 = contextlib.ExitStack()
        e3 = stk3.enter_context
        b2_b = e3(nc.sbuf_tensor("b2_bt", [128, D], F32))
        w1s = [e3(nc.sbuf_tensor(f"w1st{i}", [128, 8 * 128], BF16)) for i in range(2)]
        w2s = [e3(nc.sbuf_tensor(f"w2st{i}", [128, D], BF16)) for i in range(2)]
        hq = e3(nc.sbuf_tensor("hqt", [128, 32 * 1024], BF16))
        xlT2 = e3(nc.sbuf_tensor("xlT2t", [128, 8 * 1024], BF16))
        c_u = e3(nc.sbuf_tensor("c_ut", [128, 1024], F32))
        c_ti = e3(nc.sbuf_tensor("c_tit", [128, 1024], I32))
        c_tf = e3(nc.sbuf_tensor("c_tft", [128, 1024], F32))
        c_e = e3(nc.sbuf_tensor("c_et", [128, 1024], F32))
        c_a = e3(nc.sbuf_tensor("c_at", [128, 1024], F32))
        c_nt = e3(nc.sbuf_tensor("c_ntt", [128, 1024], F32))
        c_ti2 = e3(nc.sbuf_tensor("c_ti2t", [128, 1024], I32))
        c_ri = e3(nc.sbuf_tensor("c_rit", [128, 1024], I32))
        c_rf = e3(nc.sbuf_tensor("c_rft", [128, 1024], F32))
        c_z = e3(nc.sbuf_tensor("c_zt", [128, 1024], F32))
        c_g0 = e3(nc.sbuf_tensor("c_g0t", [128, 1024], F32))
        c_g1 = e3(nc.sbuf_tensor("c_g1t", [128, 1024], F32))
        c_y = e3(nc.sbuf_tensor("c_yt", [128, 1024], F32))
        pstk3 = contextlib.ExitStack()
        ep3 = pstk3.enter_context
        pM = [ep3(nc.psum_tensor(f"pMp{i}", [128, 512], F32)) for i in range(4)]
        pT3 = ep3(nc.psum_tensor("pT3p", [128, D], BF16))

        bcast_row(64, b2_b, pM[0])
        G["epi_v"] = cnt["v"]

        LADDER_COLS = {}

        def sign_peel(vbuf, bbuf, levels, tag, width=1024):
            K = 0.0
            hw_ = width // 2
            halves = [(0, hw_), (hw_, width)]
            t_s = [0, 0]
            t_v = [0, 0]
            wait("scalar", "v", cnt["v"])
            for i, c in enumerate(levels):
                col = LADDER_COLS[tag][i]
                for hh in (0, 1):
                    lo, hi = halves[hh]
                    if t_v[hh]:
                        wait("scalar", "v", t_v[hh])
                    t_s[hh] = op("scalar",
                                 lambda io, V=vbuf, Bb=bbuf, CL=col, L=lo, Hh=hi:
                                 io.activation(
                                     out=Bb[:, L:Hh], in_=V[:, L:Hh], func=AF.Sign,
                                     bias=ladc[:, CL:CL + 1], scale=1.0), sem="s")
                for hh in (0, 1):
                    lo, hi = halves[hh]
                    wait("vector", "s", t_s[hh])
                    t_v[hh] = op("vector",
                                 lambda io, V=vbuf, Bb=bbuf, C=c, L=lo, Hh=hi:
                                 io.scalar_tensor_tensor(
                                     out=V[:, L:Hh], in0=Bb[:, L:Hh],
                                     scalar=float(-C / 2.0), in1=V[:, L:Hh],
                                     op0=ALU.mult, op1=ALU.add), sem="v")
                    drain("vector")
                K += -c / 2.0
            wait("scalar", "v", max(t_v))
            return K

        LADDER_COLS.update({
            "hA": [0, 1, 2, 3, 4],
            "hB": [5, 6, 7],
            "h3n": [8, 9, 10, 11],
            "h4n": [12, 13, 14, 15, 16],
        })

        def ffn_chain(ft):
            wait("scalar", "v", G["chain_v"])
            for half in range(2):
                op("scalar", lambda io, HF=half, FT=ft: io.activation(
                    out=c_u[:, HF * 512:(HF + 1) * 512], in_=pM[HF][:, 0:512],
                    func=AF.Identity, bias=b1bias[:, FT:FT + 1], scale=256.0),
                   sem="s" if half == 1 else None)
            t_u = cnt["s"]
            G["pP_sfree"] = t_u
            wait("vector", "s", t_u)
            op("vector", lambda io: io.tensor_scalar(
                out=c_ti[:], in0=c_u[:], scalar1=1.0, scalar2=-16.0,
                op0=ALU.mult, op1=ALU.add))
            drain("vector")
            op("vector", lambda io: io.tensor_tensor(
                out=c_e[:], in0=c_u[:], in1=c_ti[:], op=ALU.subtract), sem="v")
            drain("vector")
            sign_peel(c_e, c_a, [16, 8, 4, 2, 1], "hA")
            op("vector", lambda io: io.tensor_scalar(
                out=c_ti[:], in0=c_u[:], scalar1=1.0 / 64, scalar2=-4.0,
                op0=ALU.mult, op1=ALU.add))
            drain("vector")
            op("vector", lambda io: io.tensor_copy(out=c_tf[:], in_=c_ti[:]))
            drain("vector")
            op("vector", lambda io: io.scalar_tensor_tensor(
                out=c_rf[:], in0=c_tf[:], scalar=-64.0, in1=c_u[:],
                op0=ALU.mult, op1=ALU.add), sem="v")
            drain("vector")
            sign_peel(c_rf, c_a, [256, 128, 64], "hB")
            t_z = op("vector", lambda io: io.tensor_tensor(
                out=c_z[:], in0=c_rf[:], in1=c_u[:], op=ALU.subtract), sem="v")
            drain("vector")
            op("vector", lambda io: io.tensor_tensor(
                out=c_rf[:], in0=c_rf[:], in1=c_e[:], op=ALU.subtract))
            drain("vector")
            op("vector", lambda io: io.tensor_scalar(
                out=c_rf[:], in0=c_rf[:], scalar1=1.0, scalar2=-208.5,
                op0=ALU.mult, op1=ALU.add))
            drain("vector")
            wait("scalar", "v", t_z)
            op("scalar", lambda io: io.activation(
                out=c_g0[:], in_=c_z[:], func=AF.Gelu, bias=cbias[:, 1:2],
                scale=-1.0 / 256))
            drain("scalar")
            t_g = op("scalar", lambda io: io.activation(
                out=c_g1[:], in_=c_z[:], func=AF.Gelu, bias=cbias[:, 2:3],
                scale=-1.0 / 256), sem="s")
            wait("vector", "s", t_g)
            op("vector", lambda io: io.tensor_tensor(
                out=c_g1[:], in0=c_g1[:], in1=c_g0[:], op=ALU.subtract))
            drain("vector")
            op("vector", lambda io: io.scalar_tensor_tensor(
                out=c_e[:], in0=c_g1[:], scalar=1.0 / 64, in1=c_rf[:],
                op0=ALU.mult, op1=ALU.mult))
            drain("vector")
            t_y = op("vector", lambda io: io.tensor_tensor(
                out=c_y[:], in0=c_e[:], in1=c_g0[:], op=ALU.add), sem="v")
            wait("scalar", "v", t_y)
            t_u2 = op("scalar", lambda io: io.activation(
                out=c_u[:], in_=c_y[:], func=AF.Identity, bias=cbias[:, 3:4],
                scale=256.0), sem="s")
            wait("vector", "s", t_u2)
            op("vector", lambda io: io.tensor_scalar(
                out=c_ti[:], in0=c_u[:], scalar1=1.0, scalar2=-8.0,
                op0=ALU.mult, op1=ALU.add))
            drain("vector")
            op("vector", lambda io: io.tensor_tensor(
                out=c_e[:], in0=c_u[:], in1=c_ti[:], op=ALU.subtract), sem="v")
            drain("vector")
            sign_peel(c_e, c_a, [8, 4, 2, 1], "h3n")
            t_ca = op("vector", lambda io: io.tensor_tensor(
                out=c_a[:], in0=c_u[:], in1=c_e[:], op=ALU.subtract), sem="v")
            drain("vector")
            wait("scalar", "v", t_ca)
            t_hq = op("scalar", lambda io, FT=ft: io.activation(
                out=hq[:, FT * 1024:(FT + 1) * 1024], in_=c_a[:],
                func=AF.Identity, bias=cbias[:, 5:6], scale=1.0 / 256),
                sem="s")
            G["chain_v"] = t_ca
            return t_hq

        wait("sync", "x2a", cnt["x2a"])
        wait("sync", "x2b", cnt["x2b"])
        for fb in range(2):
            for i in range(8):
                tile = fb * 8 + i
                sb = xstage[i % 2]
                xs = "x0" if i % 2 == 0 else "x1"
                wait("sync", "v", G["xstage_free"][i % 2])
                t_x = dma("sync", sb[:], x2_d[tile * 128:(tile + 1) * 128, :], xs)
                wait("vector", xs, t_x)
                ln_stats(sb[:], D)
                t_ap = ln_apply(sb[:], xl[:])
                G["xstage_free"][i % 2] = t_ap
                wait("tensor", "v", G["pT_free"])
                wait("tensor", "v", t_ap)
                tm = None
                for c in range(8):
                    tm = op("tensor", lambda io, cc=c: io.transpose(
                        pT3[:, cc * 128:(cc + 1) * 128],
                        xl[:, cc * 128:(cc + 1) * 128], ident[:]),
                        sem="t" if c == 7 else None)
                wait("vector", "t", G["xlT2_free"])
                wait("vector", "t", tm)
                t_ev = op("vector", lambda io, ii=i: io.tensor_copy(
                    out=xlT2[:].rearrange("p (c n) -> p c n", c=8)[:, :, ii * 128:(ii + 1) * 128],
                    in_=pT3[:].rearrange("p (c n) -> p c n", c=8)), sem="v")
                G["pT_free"] = t_ev
            t_xlT2 = cnt["v"]
            wait("scalar", "t", G["hq_free"])
            for ft in range(32):
                buf = ft % 2
                w1nm = "w1a" if buf == 0 else "w1b"
                wait("sync", "t", G["w1s_free"][buf])
                t_w1 = dma("sync", w1s[buf][:],
                           w1bf[:, ft * 1024:(ft + 1) * 1024], w1nm)
                wait("tensor", w1nm, t_w1)
                if ft == 0:
                    wait("tensor", "v", t_xlT2)
                    wait("tensor", "v", G["epi_v"])
                wait("tensor", "s", G["pP_sfree"])
                tm = None
                for half in range(2):
                    for c in range(8):
                        tm = op("tensor", lambda io, cc=c, HF=half, BUF=buf: io.matmul(
                            pM[HF][:, 0:512],
                            w1s[BUF][:, cc * 128:(cc + 1) * 128],
                            xlT2[:, cc * 1024 + HF * 512: cc * 1024 + (HF + 1) * 512],
                            start=(cc == 0), stop=(cc == 7)),
                            sem="t" if (c == 7 and half == 1) else None)
                G["w1s_free"][buf] = tm
                wait("scalar", "t", tm)
                ffn_chain(ft)
            t_hq_all = cnt["s"]
            G["xlT2_free"] = cnt["t"]
            for tp in range(4):
                tiles = (fb * 8 + tp * 2, fb * 8 + tp * 2 + 1)
                t_w2mm = None
                for f in range(32):
                    buf = f % 2
                    w2nm = "w2a" if buf == 0 else "w2b"
                    wait("gpsimd", "t", G["w2s_free"][buf])
                    t_w2 = dma("gpsimd", w2s[buf][:],
                               w2bf[:, f * 1024:(f + 1) * 1024], w2nm)
                    wait("tensor", w2nm, t_w2)
                    if f == 0:
                        wait("tensor", "s", t_hq_all)
                        wait("tensor", "s", G["pP_sfree"])
                        wait("tensor", "v", G["epi_v"])
                    tm = None
                    for li in range(2):
                        for bank in range(2):
                            tm = op("tensor",
                                    lambda io, FF_=f, LI=li, BK=bank, TL=tiles[li], FB=fb, BUF=buf:
                                    io.matmul(
                                        pM[LI * 2 + BK][:, 0:512],
                                        hq[:, FF_ * 1024 + (TL - FB * 8) * 128:
                                           FF_ * 1024 + (TL - FB * 8) * 128 + 128],
                                        w2s[BUF][:, BK * 512:(BK + 1) * 512],
                                        start=(FF_ == 0), stop=(FF_ == 31)),
                                    sem="t" if (li == 1 and bank == 1) else None)
                    t_w2mm = tm
                    G["w2s_free"][buf] = tm
                for li in range(2):
                    tile = tiles[li]
                    wait("vector", "t", t_w2mm)
                    wait("vector", "v", G["wosum_free"])
                    for bank in range(2):
                        op("vector", lambda io, LI=li, BK=bank: io.scalar_tensor_tensor(
                            out=wosum[:, BK * 512:(BK + 1) * 512],
                            in0=pM[LI * 2 + BK][:, 0:512],
                            scalar=1.0, in1=b2_b[:, BK * 512:(BK + 1) * 512],
                            op0=ALU.mult, op1=ALU.add),
                           sem="v" if bank == 1 else None)
                    drain("vector")
                    t_o = cnt["v"]
                    if li == 1:
                        G["epi_v"] = t_o
                    wait("scalar", "v", t_o)
                    t_u3 = op("scalar", lambda io: io.activation(
                        out=c_u[:], in_=wosum[:], func=AF.Identity,
                        bias=cbias[:, 4:5], scale=256.0), sem="s")
                    wait("vector", "s", t_u3)
                    op("vector", lambda io: io.tensor_scalar(
                        out=c_ti[:], in0=c_u[:], scalar1=1.0, scalar2=-16.0,
                        op0=ALU.mult, op1=ALU.add))
                    drain("vector")
                    op("vector", lambda io: io.tensor_tensor(
                        out=c_e[:], in0=c_u[:], in1=c_ti[:], op=ALU.subtract), sem="v")
                    drain("vector")
                    sign_peel(c_e, c_a, [16, 8, 4, 2, 1], "h4n")
                    op("vector", lambda io: io.tensor_tensor(
                        out=c_a[:], in0=c_u[:], in1=c_e[:], op=ALU.subtract))
                    drain("vector")
                    op("vector", lambda io: io.tensor_scalar(
                        out=lntmp[:], in0=c_a[:], scalar1=1.0 / 256,
                        scalar2=-31.939453125, op0=ALU.mult, op1=ALU.add))
                    drain("vector")
                    ln_stats(lntmp[:], D)
                    ln_apply(lntmp[:], wosum[:])
                    sb = xstage[li]
                    xs = "x0" if li == 0 else "x1"
                    wait("sync", "v", G["xstage_free"][li])
                    t_x = dma("sync", sb[:], x2_d[tile * 128:(tile + 1) * 128, :], xs)
                    wait("vector", xs, t_x)
                    t_out = op("vector", lambda io, LI=li: io.tensor_tensor(
                        out=outt[LI][:], in0=xstage[LI][:], in1=wosum[:],
                        op=ALU.add), sem="v")
                    drain("vector")
                    G["xstage_free"][li] = t_out
                    G["wosum_free"] = t_out
                    out_nm = "outa" if li == 0 else "outb"
                    wait("scalar", "v", t_out)
                    t_st = dma("scalar", out_d[tile * 128:(tile + 1) * 128, :],
                               outt[li][:], out_nm)
                    wait("vector", out_nm, t_st)
            G["hq_free"] = cnt["t"]
        wait("gpsimd", "outa", cnt["outa"])
        wait("gpsimd", "outb", cnt["outb"])
        pstk3.close()
        stk3.close()

        @BLOCK.sync
        def _(io):
            for f in ops["sync"]:
                f(io)

        @BLOCK.tensor
        def _(io):
            for f in ops["tensor"]:
                f(io)

        @BLOCK.vector
        def _(io):
            for f in ops["vector"]:
                f(io)

        @BLOCK.scalar
        def _(io):
            for f in ops["scalar"]:
                f(io)

        @BLOCK.gpsimd
        def _(io):
            for f in ops["gpsimd"]:
                f(io)

    return nc


_NC_CACHE = {}


def _constants(par):
    ident = np.eye(128, dtype=ml_dtypes.bfloat16)
    kposc = (np.arange(NT)[None, :] * 128 + np.arange(128)[:, None]).astype(np.float32)
    qshift = np.zeros((128, H * NG), dtype=np.float32)
    for h in range(H):
        for g in range(NG):
            t0 = 128 * (8 * g + par)
            t1 = 128 * (8 * g + 6 + par) + 127
            qref = 0.5 * (t0 + t1)
            qshift[:, h * NG + g] = -MSHIFT - SLOPES[h] * qref / 8.0
    ladc = np.zeros((128, 64), dtype=np.float32)
    col = 0
    for levels in ([16, 8, 4, 2, 1], [256, 128, 64], [8, 4, 2, 1],
                   [16, 8, 4, 2, 1]):
        K = 0.0
        for c in levels:
            ladc[:, col] = -(c + K)
            K += c / 2.0
            col += 1
    qpr = np.zeros((128, 512), dtype=np.float32)
    qpr[64, :] = np.repeat(np.array([-384.0, -128.0, 128.0, 384.0], dtype=np.float32), 128)
    lowtri = (np.arange(128)[:, None] <= np.arange(128)[None, :])
    masks = np.zeros((128, 256), dtype=ml_dtypes.bfloat16)
    if par == 0:
        masks[:, 0:128] = lowtri.astype(ml_dtypes.bfloat16)
        masks[:, 128:256] = 0
    else:
        masks[:, 0:128] = 1
        masks[:, 128:256] = lowtri.astype(ml_dtypes.bfloat16)
    return ident, kposc, qshift, masks, qpr, ladc


def kernel(L0_attn_x, L0_attn_Wq, L0_attn_Wk, L0_attn_Wv, L0_attn_Wo, L0_attn_bo,
           L0_ffn_w1, L0_ffn_b1, L0_ffn_w2, L0_ffn_b2):
    x = np.ascontiguousarray(np.asarray(L0_attn_x, dtype=np.float32))
    Wq = np.ascontiguousarray(np.asarray(L0_attn_Wq, dtype=np.float32))
    Wk = np.ascontiguousarray(np.asarray(L0_attn_Wk, dtype=np.float32))
    Wv = np.ascontiguousarray(np.asarray(L0_attn_Wv, dtype=np.float32))
    Wo = np.ascontiguousarray(np.asarray(L0_attn_Wo, dtype=np.float32))
    bo = np.ascontiguousarray(np.asarray(L0_attn_bo, dtype=np.float32))
    w1 = np.ascontiguousarray(np.asarray(L0_ffn_w1, dtype=np.float32))
    b1 = np.ascontiguousarray(
        np.asarray(L0_ffn_b1, dtype=np.float32).reshape(FF // 128, 128).T)
    w2 = np.ascontiguousarray(np.asarray(L0_ffn_w2, dtype=np.float32))
    b2 = np.ascontiguousarray(np.asarray(L0_ffn_b2, dtype=np.float32))

    if "nc" not in _NC_CACHE:
        _NC_CACHE["nc"] = build_nc()
    nc = _NC_CACHE["nc"]

    in_maps = []
    for c in range(8):
        b, par = c // 2, c % 2
        xb = x[b]
        x_my = np.ascontiguousarray(
            xb.reshape(NT, 128, D)[par::2].reshape(MYN, D))
        ident, kposc, qshift, masks, qpr, ladc = _constants(par)
        in_maps.append({
            "x_full": xb, "x_my": x_my,
            "Wq": Wq, "Wk": Wk, "Wv": Wv, "Wo": Wo, "bo": bo,
            "w1": w1, "b1": b1, "w2": w2, "b2": b2,
            "identc": ident, "kposc": kposc, "qshiftc": qshift, "masksc": masks,
            "qprc": qpr, "ladc": ladc,
        })
    res = run_bass_kernel_spmd(nc, in_maps, core_ids=list(range(8)))
    out = np.empty((B, N, D), dtype=np.float32)
    for c in range(8):
        b, par = c // 2, c % 2
        o = res.results[c]["out"].reshape(MYT, 128, D)
        out[b].reshape(NT, 128, D)[par::2] = o
    return out
